# version 21
# baseline (speedup 1.0000x reference)
import numpy as np
import ml_dtypes
from contextlib import ExitStack

import concourse.bass as bass
import concourse.mybir as mybir
from concourse.bass_utils import run_bass_kernel_spmd

F32 = mybir.dt.float32
BF16 = mybir.dt.bfloat16
AF = mybir.ActivationFunctionType
ALU = mybir.AluOpType
AX = mybir.AxisListType

B, T, C = 4, 2048, 2048
FC = 16
INTER, IC = 5632, 44
QR, QC = 1536, 12
KVR, KC = 512, 4
H, DN, DR, D = 16, 128, 64, 192
TOK = 1024
TGS = 512
NG = 2
NKB = 16
SCALE = float(1.0 / np.sqrt(192.0))
EPS = 1e-5
NEGBIG = -1.0e30
MARGIN = 1.0625
MSHIFT = 40.0
EXT = {0: 8, 1: 16}
DIAG = {0: 0, 1: 12}

ENGINES = ("sync", "gpsimd", "tensor", "vector", "scalar")
STREAMS = ("XC", "X0", "X1", "ST", "G0", "G1", "D0", "D1", "L0", "L1",
           "A0", "A1", "WO", "W1", "W2", "CW0", "CW1", "CD0", "CD1", "RG0", "RG1", "RD0", "RD1", "PE", "ACT", "DVE")


class Prog:
    def __init__(self):
        self.ops = {e: [] for e in ENGINES}
        self.n = {s: 0 for s in STREAMS}
        self.hw = {e: {s: 0 for s in STREAMS} for e in ENGINES}
        self.sems = {}

    def emit(self, eng, fn):
        self.ops[eng].append(fn)

    def wait(self, eng, stream, val):
        if val <= 0 or val <= self.hw[eng][stream]:
            return
        self.hw[eng][stream] = val
        self.emit(eng, lambda e, s=stream, v=val: e.wait_ge(self.sems[s], v))

    def barrier(self, engs=ENGINES, streams=("PE", "ACT", "DVE")):
        for eng in engs:
            for s in streams:
                self.wait(eng, s, self.n[s])

    def bump(self, stream, k=1):
        self.n[stream] += k
        return self.n[stream]


def _bump_inc(P, inc):
    if inc is None:
        return None
    P.bump(inc[0], inc[1])
    return (inc[0], inc[1])


def _attach(ins, P, inc):
    if inc is not None:
        ins.then_inc(P.sems[inc[0]], inc[1])


def mm(P, out, lhsT, rhs, start, stop, inc=None):
    inc = _bump_inc(P, inc)

    def f(e, out=out, l=lhsT, r=rhs, st=start, sp=stop, inc=inc, P=P):
        _attach(e.matmul(out, l, r, start=st, stop=sp), P, inc)
    P.emit("tensor", f)


def act(P, out, in_, func, bias=0.0, scale=1.0, inc=None):
    inc = _bump_inc(P, inc)

    def f(e, out=out, i=in_, fu=func, b=bias, s=scale, inc=inc, P=P):
        _attach(e.activation(out, i, fu, bias=b, scale=s), P, inc)
    P.emit("scalar", f)


def acopy(P, out, in_, inc=None):
    inc = _bump_inc(P, inc)

    def f(e, out=out, i=in_, inc=inc, P=P):
        _attach(e.copy(out, i), P, inc)
    P.emit("scalar", f)


def vop(P, name, *args, inc=None, **kw):
    inc = _bump_inc(P, inc)

    def f(e, n=name, a=args, kw=kw, inc=inc, P=P):
        _attach(getattr(e, n)(*a, **kw), P, inc)
    P.emit("vector", f)


def dma(P, eng, out, in_, stream):
    v = P.bump(stream, 16)

    def f(e, out=out, i=in_, P=P, s=stream):
        e.dma_start(out=out, in_=i).then_inc(P.sems[s], 16)
    P.emit(eng, f)
    return v


def build_program(upto="all", dbg=False, race_detect=True):
    nc = bass.Bass("TRN2", target_bir_lowering=False,
                   detect_race_conditions=race_detect)
    P = Prog()

    d_xo = nc.dram_tensor("xo", [FC, 128, TOK], F32, kind="ExternalInput").ap()
    d_xe = nc.dram_tensor("xe", [FC, 128, TOK], F32, kind="ExternalInput").ap()
    d_gw = nc.dram_tensor("gw", [IC, 128, FC * 128], F32, kind="ExternalInput").ap()
    d_uw = nc.dram_tensor("uw", [IC, 128, FC * 128], F32, kind="ExternalInput").ap()
    d_dw = nc.dram_tensor("dw", [FC, 128, IC * 128], F32, kind="ExternalInput").ap()
    d_gs = nc.dram_tensor("gsc", [128, IC], F32, kind="ExternalInput").ap()
    d_us = nc.dram_tensor("usc", [128, IC], F32, kind="ExternalInput").ap()
    d_ds = nc.dram_tensor("dsc", [128, FC], F32, kind="ExternalInput").ap()
    d_wqd = nc.dram_tensor("wqd", [QC, 128, FC * 128], F32, kind="ExternalInput").ap()
    d_wkvd = nc.dram_tensor("wkvd", [KC, 128, FC * 128], F32, kind="ExternalInput").ap()
    d_aw = nc.dram_tensor("aw", [H, 128, 3840], F32, kind="ExternalInput").ap()
    d_wo = nc.dram_tensor("wo", [FC, 128, 3072], F32, kind="ExternalInput").ap()
    d_ktab = nc.dram_tensor("ktab", [128, 2048], BF16, kind="ExternalInput").ap()
    d_qtab = nc.dram_tensor("qtab", [128, TOK], BF16, kind="ExternalInput").ap()
    d_tri = nc.dram_tensor("tri", [NG, 128, 4 * TGS], BF16, kind="ExternalInput").ap()
    d_badd = nc.dram_tensor("badd", [128, 32], F32, kind="ExternalInput").ap()
    d_y = nc.dram_tensor("y", [FC, 128, TOK], F32, kind="ExternalOutput").ap()
    c_gw = nc.dram_tensor("c_gw", [IC, 128, FC * 128], BF16, kind="Internal").ap()
    c_uw = nc.dram_tensor("c_uw", [IC, 128, FC * 128], BF16, kind="Internal").ap()
    c_dw = nc.dram_tensor("c_dw", [FC, 128, IC * 128], BF16, kind="Internal").ap()
    if dbg:
        d_dkv = nc.dram_tensor("dbg_kv", [KC, 128, 2048], BF16, kind="ExternalOutput").ap()
        d_dql = nc.dram_tensor("dbg_ql", [QC, 128, TOK], BF16, kind="ExternalOutput").ap()
        d_dat = nc.dram_tensor("dbg_at", [128, 8192], F32, kind="ExternalOutput").ap()
        d_dao = nc.dram_tensor("dbg_ao", [128, 12288], BF16, kind="ExternalOutput").ap()

    B0 = 16512
    assert B0 >= nc.sbuf_base
    TOPB = nc.sbuf_top

    def S(name, shape, dtype, off):
        assert off % 32 == 0, name
        nbytes = shape[1] * (4 if dtype == F32 else 2)
        assert B0 <= off and off + nbytes <= TOPB, (name, off, nbytes, off + nbytes)
        return nc.alloc_sbuf_tensor_at(name, shape, dtype, offset=off)

    ones_bf = S("ones_bf", [128, 128], BF16, B0 + 0)
    ones_f = S("ones_f", [128, 128], F32, B0 + 256)
    t_gs = S("t_gs", [128, IC], F32, B0 + 768)
    t_us = S("t_us", [128, IC], F32, B0 + 960)
    t_ds = S("t_ds", [128, FC], F32, B0 + 1152)
    t_hds = S("t_hds", [128, FC], F32, B0 + 1216)
    t_badd = S("t_badd", [128, 32], F32, B0 + 1280)
    scr0 = S("scr0", [128, 512], F32, B0 + 1408)
    scr1 = S("scr1", [128, 512], F32, B0 + 3456)
    scrb0 = S("scrb0", [128, 1024], BF16, B0 + 1408)
    scrb1 = S("scrb1", [128, 1024], BF16, B0 + 3456)
    scrb0b = S("scrb0b", [128, 1024], BF16, B0 + 13696)
    scrb1b = S("scrb1b", [128, 1024], BF16, B0 + 15744)
    mrow = S("mrow", [1, 512], F32, B0 + 5504)
    vrow = S("vrow", [1, 512], F32, B0 + 7552)
    brow = S("brow", [1, 512], F32, B0 + 9600)
    trow = S("trow", [1, 512], F32, B0 + 11648)
    lrow = S("lrow", [1, 512], F32, B0 + 13696)
    irow = S("irow", [1, 512], F32, B0 + 15744)
    kvlat = S("kvlat", [128, KC * 2048], BF16, B0 + 17792)

    A0 = B0 + 34176

    xstrm = S("xstrm", [128, 2048], F32, A0 + 0)
    resid = S("resid", [128, FC * TOK], F32, A0 + 8192)
    xn = S("xn", [128, FC * TOK], BF16, A0 + 8192)
    gbuf = S("gbuf", [128, 4096], BF16, A0 + 40960)
    ubuf = S("ubuf", [128, 4096], BF16, A0 + 49152)
    abc = S("abc", [128, TOK], F32, A0 + 57344)
    bbc = S("bbc", [128, TOK], F32, A0 + 61440)
    hdn = S("hdn", [128, IC * TOK], BF16, A0 + 73728)
    dbuf = S("dbuf", [128, IC * 128], BF16, A0 + 163840)
    abc2 = S("abc2", [128, TOK], F32, A0 + 0)
    bbc2 = S("bbc2", [128, TOK], F32, A0 + 4096)

    xn2 = S("xn2", [128, FC * TOK], BF16, A0 + 73728)
    latbuf = S("latbuf", [128, 4096], BF16, A0 + 106496)
    qlat = S("qlat", [128, QC * TOK], BF16, A0 + 114688)

    ktab = S("ktab", [128, 2048], BF16, A0 + 73728)
    qtab = S("qtab", [128, TOK], BF16, A0 + 77824)
    tri_g = S("tri_g", [128, 2048], BF16, A0 + 79872)
    kh_a = S("kh_a", [128, 2048], BF16, A0 + 83968)
    kh_b = S("kh_b", [64, 2048], BF16, A0 + 88064)
    qh_a = S("qh_a", [128, 512], BF16, A0 + 92160)
    qh_b = S("qh_b", [64, 512], BF16, A0 + 93184)
    VST = 194
    vh = S("vh", [128, NKB * VST], BF16, A0 + 94208)
    vh2 = S("vh2", [128, NKB * VST], BF16, A0 + 100416)
    ropeA = S("ropeA", [64, 512], F32, A0 + 106624)
    ropeB = S("ropeB", [64, 512], F32, A0 + 108672)
    invlb = S("invlb", [128, 512], F32, A0 + 110720)
    wob = S("wob", [128, 3072], BF16, A0 + 83968)
    wob1 = S("wob1", [128, 3072], BF16, A0 + 163840)
    wob2 = S("wob2", [128, 3072], BF16, A0 + 94208)
    aog = S("aog", [128, 24 * TGS], BF16, A0 + 139264)
    awb0 = S("awb0", [128, 3840], BF16, A0 + 163840)
    ptile = S("ptile", [128, 3 * 512], BF16, A0 + 171520)
    awb1 = S("awb1", [128, 3840], BF16, A0 + 0)

    xn3 = S("xn3", [128, FC * TOK], BF16, A0 + 73728)
    gbuf2 = S("gbuf2", [128, 4096], BF16, A0 + 106496)
    ubuf2 = S("ubuf2", [128, 4096], BF16, B0 + 17792)
    hdn2 = S("hdn2", [128, IC * TGS], BF16, A0 + 114688)
    dbuf2 = S("dbuf2", [128, IC * 128], BF16, A0 + 159744)

    ps = [nc.alloc_psum_tensor(f"ps{i}", [128, 512], F32) for i in range(8)]
    nc._dbg_tensors = dict(resid=resid, xn=xn, abc=abc, bbc=bbc, hdn=hdn, xn2=xn2,
                           kvlat=kvlat, qlat=qlat, scr0=scr0, scr1=scr1, mrow=mrow,
                           vrow=vrow, brow=brow, xstrm=xstrm, kh_a=kh_a, kh_b=kh_b,
                           qh_a=qh_a, qh_b=qh_b, vh=vh, vh2=vh2, aog=aog,
                           invlb=invlb, xn3=xn3, hdn2=hdn2, ptile=ptile)

    root = ExitStack()
    for s in STREAMS:
        P.sems[s] = root.enter_context(nc.semaphore(s))
    root.enter_context(nc.allow_low_precision("bf16 matmuls allowed by problem tolerance"))

    for dst, dsrc in ((t_gs, d_gs), (t_us, d_us), (t_ds, d_ds), (t_badd, d_badd)):
        dma(P, "sync", dst[:, :], dsrc, "XC")
    xh_consts = P.n["XC"]
    vop(P, "memset", ones_bf[:, :], 1.0)
    vop(P, "memset", ones_f[:, :], 1.0, inc=("DVE", 1))
    P.wait("vector", "XC", xh_consts)
    vop(P, "tensor_scalar_mul", t_hds[:, :], t_ds[:, :], 0.5, inc=("DVE", 1))

    def emit_ln(d_src, src_t, xn_t, a_t, b_t, src_kind, ready=None):
        P.wait("scalar", "XC", xh_consts)
        pe_b = P.n["PE"]
        act_b = P.n["ACT"]
        dve_b = P.n["DVE"]
        dve_entry = P.n["DVE"]
        for fc in range(FC):
            if src_kind == "hbm":
                if fc >= 2:
                    P.wait("sync", "PE", pe_b + fc - 1)
                else:
                    P.wait("sync", "DVE", dve_entry)
                xv = dma(P, "sync", xstrm[:, (fc % 2) * 1024:(fc % 2) * 1024 + 1024],
                         d_src[fc], f"X{fc % 2}")
                P.wait("tensor", f"X{fc % 2}", xv)
                P.wait("scalar", f"X{fc % 2}", xv)
                P.wait("vector", f"X{fc % 2}", xv)
                xc = xstrm[:, (fc % 2) * 1024:(fc % 2) * 1024 + 1024]
            else:
                if ready is not None:
                    st_, v_ = ready[fc]
                    P.wait("scalar", st_, v_)
                    P.wait("vector", st_, v_)
                    P.wait("tensor", st_, v_)
                xc = src_t[:, fc * TOK:(fc + 1) * TOK]
            sq = scrb0 if fc % 2 == 0 else scrb0b
            sx = scrb1 if fc % 2 == 0 else scrb1b
            if fc >= 2:
                P.wait("scalar", "PE", pe_b + fc - 1)
                P.wait("vector", "PE", pe_b + fc - 1)
            act(P, sq[:, 0:512], xc[:, 0:512], AF.Square)
            act(P, sq[:, 512:1024], xc[:, 512:1024], AF.Square, inc=("ACT", 1))
            vop(P, "tensor_copy", sx[:, :], xc[:, :], inc=("DVE", 1))
            P.wait("tensor", "ACT", act_b + fc + 1)
            P.wait("tensor", "DVE", dve_b + fc + 1)
            for tg in range(2):
                mm(P, ps[4 + tg][0:1, :], ones_bf[:, 0:1],
                   sx[:, tg * 512:(tg + 1) * 512],
                   start=(fc == 0), stop=(fc == FC - 1))
                mm(P, ps[6 + tg][0:1, :], ones_bf[:, 0:1],
                   sq[:, tg * 512:(tg + 1) * 512],
                   start=(fc == 0), stop=(fc == FC - 1),
                   inc=("PE", 1) if tg == 1 else None)
        pe_stats = P.n["PE"]
        for tg in range(2):
            P.wait("vector", "PE", pe_stats)
            vop(P, "tensor_scalar_mul", mrow[:, :], ps[4 + tg][0:1, :], 1.0 / C)
            vop(P, "tensor_scalar", vrow[:, :], ps[6 + tg][0:1, :], 1.0 / C, EPS,
                ALU.mult, ALU.add)
            vop(P, "tensor_mul", trow[:, :], mrow[:, :], mrow[:, :])
            vop(P, "tensor_sub", vrow[:, :], vrow[:, :], trow[:, :], inc=("DVE", 1))
            P.wait("scalar", "DVE", P.n["DVE"])
            act(P, trow[:, :], vrow[:, :], AF.Ln)
            act(P, vrow[:, :], trow[:, :], AF.Exp, scale=-0.5, inc=("ACT", 1))
            P.wait("vector", "ACT", P.n["ACT"])
            vop(P, "scalar_tensor_tensor", brow[:, :], mrow[:, :], -1.0, vrow[:, :],
                ALU.mult, ALU.mult, inc=("DVE", 1))
            P.wait("tensor", "DVE", P.n["DVE"])
            mm(P, ps[0 + tg][:, :], ones_f[0:1, 0:128], vrow[:, :], start=True, stop=True)
            mm(P, ps[2 + tg][:, :], ones_f[0:1, 0:128], brow[:, :], start=True, stop=True,
               inc=("PE", 1))
            P.wait("vector", "PE", P.n["PE"])
            vop(P, "tensor_copy", a_t[:, tg * 512:(tg + 1) * 512], ps[0 + tg][:, :])
            vop(P, "tensor_copy", b_t[:, tg * 512:(tg + 1) * 512], ps[2 + tg][:, :],
                inc=("DVE", 1))
        dve_ab = P.n["DVE"]
        for fc in range(FC):
            if src_kind == "hbm":
                if fc < 2:
                    P.wait("sync", "PE", pe_stats)
                else:
                    P.wait("sync", "DVE", dve_ab + fc - 1)
                xv = dma(P, "sync", xstrm[:, (fc % 2) * 1024:(fc % 2) * 1024 + 1024],
                         d_src[fc], f"X{fc % 2}")
                P.wait("vector", f"X{fc % 2}", xv)
                xc = xstrm[:, (fc % 2) * 1024:(fc % 2) * 1024 + 1024]
            else:
                if ready is not None:
                    st_, v_ = ready[fc]
                    P.wait("vector", st_, v_)
                xc = src_t[:, fc * TOK:(fc + 1) * TOK]
            for tg in range(2):
                xs = xc[:, tg * 512:(tg + 1) * 512]
                sl = (scr0 if tg == 0 else scr1)
                vop(P, "tensor_mul", sl[:, :], xs, a_t[:, tg * 512:(tg + 1) * 512])
                vop(P, "tensor_add",
                    xn_t[:, fc * TOK + tg * 512: fc * TOK + tg * 512 + 512],
                    sl[:, :], b_t[:, tg * 512:(tg + 1) * 512],
                    inc=("DVE", 1) if tg == 1 else None)
        return dve_ab

    def emit_ffn_preload(gb, ub, wc, pe_guard=None, dve_guard=None):
        eng_ld = "sync" if wc == "read" else "gpsimd"
        P.wait(eng_ld, "PE", P.n["PE"] if pe_guard is None else pe_guard)
        if dve_guard is not None:
            P.wait(eng_ld, "DVE", dve_guard)
        gu_v = {}
        cw_v = {}
        for ic in (0, 1):
            slot = ic % 2
            if wc == "read":
                dma(P, "sync", gb[:, slot * 2048:slot * 2048 + 2048],
                    c_gw[ic], f"RG{slot}")
                gu_v[ic] = dma(P, "sync", ub[:, slot * 2048:slot * 2048 + 2048],
                               c_uw[ic], f"RG{slot}")
            else:
                dma(P, "gpsimd", gb[:, slot * 2048:slot * 2048 + 2048], d_gw[ic],
                    f"G{slot}")
                gu_v[ic] = dma(P, "gpsimd", ub[:, slot * 2048:slot * 2048 + 2048],
                               d_uw[ic], f"G{slot}")
                if wc == "write":
                    P.wait("sync", f"G{slot}", gu_v[ic])
                    dma(P, "sync", c_gw[ic], gb[:, slot * 2048:slot * 2048 + 2048],
                        f"CW{slot}")
                    cw_v[ic] = dma(P, "sync", c_uw[ic],
                                   ub[:, slot * 2048:slot * 2048 + 2048], f"CW{slot}")
        return (gu_v, cw_v)

    def emit_ffn(xn_t, hdn_t, gb, ub, db, tgs, d_xsrc, mode, xn_base, wc=None,
                 pre=None):
        eng_ld = "sync" if wc == "read" else "gpsimd"
        P.wait(eng_ld, "PE", P.n["PE"])
        P.wait("scalar", "XC", xh_consts)
        if wc == "read":
            for s_ in ("CW0", "CW1", "CD0", "CD1"):
                P.wait("sync", s_, P.n[s_])
        ntg = len(tgs)
        W = ntg * 512
        act_ic = {}
        dve_ic = {}
        pe_idx = {}
        gu_v = {}
        cw_v = {}

        def load_gu(ic):
            slot = ic % 2
            if wc == "read":
                dma(P, "sync", gb[:, slot * 2048:slot * 2048 + 2048],
                    c_gw[ic], f"RG{slot}")
                gu_v[ic] = dma(P, "sync", ub[:, slot * 2048:slot * 2048 + 2048],
                               c_uw[ic], f"RG{slot}")
                return
            sem = f"G{slot}"
            dma(P, "gpsimd", gb[:, slot * 2048:slot * 2048 + 2048], d_gw[ic], sem)
            gu_v[ic] = dma(P, "gpsimd", ub[:, slot * 2048:slot * 2048 + 2048],
                           d_uw[ic], sem)
            if wc == "write":
                P.wait("sync", sem, gu_v[ic])
                dma(P, "sync", c_gw[ic], gb[:, slot * 2048:slot * 2048 + 2048],
                    f"CW{slot}")
                cw_v[ic] = dma(P, "sync", c_uw[ic],
                               ub[:, slot * 2048:slot * 2048 + 2048], f"CW{slot}")

        if pre is None:
            load_gu(0)
            load_gu(1)
        else:
            gu_v.update(pre[0])
            cw_v.update(pre[1])
        for ic in range(IC):
            bsel = (ic % 2) * 4
            slot = ic % 2
            P.wait("tensor", f"RG{slot}" if wc == "read" else f"G{slot}", gu_v[ic])
            if ic >= 2:
                P.wait("tensor", "ACT", act_ic[ic - 2])
                P.wait("tensor", "DVE", dve_ic[ic - 2])
            for kg in range(16):
                P.wait("tensor", "DVE", xn_base + kg + 1)
                start = (kg == 0)
                stop = (kg == 15)
                for ti in range(ntg):
                    tg = tgs[ti]
                    xs = xn_t[:, kg * TOK + tg * 512: kg * TOK + tg * 512 + 512]
                    mm(P, ps[bsel + ti][:, :],
                       gb[:, slot * 2048 + kg * 128: slot * 2048 + (kg + 1) * 128],
                       xs, start=start, stop=stop)
                    mm(P, ps[bsel + 2 + ti][:, :],
                       ub[:, slot * 2048 + kg * 128: slot * 2048 + (kg + 1) * 128],
                       xs, start=start, stop=stop,
                       inc=("PE", 1) if (kg == 15 and ti == ntg - 1) else None)
            pe_idx[ic] = P.n["PE"]
            if ic + 2 < IC:
                eng_pf = "sync" if wc == "read" else "gpsimd"
                P.wait(eng_pf, "PE", pe_idx[ic])
                if wc == "write":
                    P.wait("gpsimd", f"CW{slot}", cw_v[ic])
                load_gu(ic + 2)
            pe_ic = P.n["PE"]
            P.wait("scalar", "PE", pe_ic)
            if ic >= 1:
                P.wait("scalar", "DVE", dve_ic[ic - 1])
            for ti in range(ntg):
                act(P, (scr0 if ti == 0 else scr1)[:, :], ps[bsel + ti][:, :], AF.Silu,
                    scale=t_gs[:, ic:ic + 1], inc=("ACT", 1) if ti == ntg - 1 else None)
            act_ic[ic] = P.n["ACT"]
            P.wait("vector", "ACT", act_ic[ic])
            for ti in range(ntg):
                vop(P, "scalar_tensor_tensor",
                    hdn_t[:, ic * W + ti * 512: ic * W + (ti + 1) * 512],
                    ps[bsel + 2 + ti][:, :], t_us[:, ic:ic + 1],
                    (scr0 if ti == 0 else scr1)[:, :],
                    ALU.mult, ALU.mult, inc=("DVE", 1) if ti == ntg - 1 else None)
            dve_ic[ic] = P.n["DVE"]
        dve_hdn = P.n["DVE"]
        pe_d = {}
        dve_res = {}
        d_v = {}

        cd_v = {}

        def load_d(mc, hc2):
            idx = mc * 2 + hc2
            slot = idx % 2
            if wc == "read":
                d_v[idx] = dma(P, "sync", db[:, slot * 2816:slot * 2816 + 2816],
                               c_dw[mc, :, hc2 * 2816:hc2 * 2816 + 2816], f"RD{slot}")
                return
            d_v[idx] = dma(P, "gpsimd", db[:, slot * 2816:slot * 2816 + 2816],
                           d_dw[mc, :, hc2 * 2816:hc2 * 2816 + 2816], f"D{slot}")
            if wc == "write":
                P.wait("sync", f"D{slot}", d_v[idx])
                cd_v[idx] = dma(P, "sync", c_dw[mc, :, hc2 * 2816:hc2 * 2816 + 2816],
                                db[:, slot * 2816:slot * 2816 + 2816], f"CD{slot}")

        load_d(0, 0)
        load_d(0, 1)
        for mc in range(FC):
            bsel = (mc % 2) * 2
            for hc2 in range(2):
                idx = mc * 2 + hc2
                P.wait("tensor", f"RD{idx % 2}" if wc == "read" else f"D{idx % 2}",
                       d_v[idx])
                if idx == 0:
                    P.wait("tensor", "DVE", dve_hdn)
                if mc >= 2 and hc2 == 0:
                    P.wait("tensor", "DVE", dve_res[mc - 2])
                slot = idx % 2
                for kc in range(22):
                    kg = hc2 * 22 + kc
                    start = (hc2 == 0 and kc == 0)
                    stop = (hc2 == 1 and kc == 21)
                    for ti in range(ntg):
                        mm(P, ps[bsel + ti][:, :],
                           db[:, slot * 2816 + kc * 128: slot * 2816 + (kc + 1) * 128],
                           hdn_t[:, kg * W + ti * 512: kg * W + (ti + 1) * 512],
                           start=start, stop=stop,
                           inc=("PE", 1) if (kc == 21 and ti == ntg - 1) else None)
                pe_d[idx] = P.n["PE"]
                if idx + 2 < 2 * FC:
                    eng_pf = "sync" if wc == "read" else "gpsimd"
                    P.wait(eng_pf, "PE", pe_d[idx])
                    if wc == "write":
                        P.wait("gpsimd", f"CD{idx % 2}", cd_v[idx])
                    load_d((idx + 2) // 2, (idx + 2) % 2)
            pe_mc = P.n["PE"]
            if mode == "stream":
                if mc < 2:
                    P.wait("sync", "DVE", dve_hdn)
                else:
                    P.wait("sync", "DVE", dve_res[mc - 2])
                xv = dma(P, "sync", xstrm[:, (mc % 2) * 1024:(mc % 2) * 1024 + 1024],
                         d_xsrc[mc], f"X{mc % 2}")
                P.wait("vector", f"X{mc % 2}", xv)
            P.wait("vector", "PE", pe_mc)
            for ti in range(ntg):
                tg = tgs[ti]
                dst = resid[:, mc * TOK + tg * 512: mc * TOK + tg * 512 + 512]
                if mode == "stream":
                    other = xstrm[:, (mc % 2) * 1024 + tg * 512:
                                  (mc % 2) * 1024 + tg * 512 + 512]
                else:
                    other = dst
                vop(P, "scalar_tensor_tensor", dst, ps[bsel + ti][:, :],
                    t_hds[:, mc:mc + 1], other, ALU.mult, ALU.add,
                    inc=("DVE", 1) if ti == ntg - 1 else None)
            dve_res[mc] = P.n["DVE"]
        return dve_res, pe_idx[IC - 1]

    def emit_lat_preload(d_wsrc, MCH):
        P.wait("gpsimd", "PE", P.n["PE"])
        l_v = {}
        for m in range(min(2, MCH)):
            l_v[m] = dma(P, "gpsimd", latbuf[:, (m % 2) * 2048:(m % 2) * 2048 + 2048],
                         d_wsrc[m], f"L{m % 2}")
        return l_v

    def emit_lat(xn_t, xn_base, d_wsrc, MCH, dest_t, dest_off, pre=None):
        consumed = {}
        act_ev = {}
        if pre is None:
            P.wait("gpsimd", "PE", P.n["PE"])
            l_v = {}
        else:
            l_v = pre

        def load_l(m):
            slot = m % 2
            l_v[m] = dma(P, "gpsimd", latbuf[:, slot * 2048:slot * 2048 + 2048],
                         d_wsrc[m], f"L{slot}")

        if pre is None:
            load_l(0)
            if MCH > 1:
                load_l(1)
        for m in range(MCH):
            slot = m % 2
            P.wait("tensor", f"L{slot}", l_v[m])
            bsel = (m % 2) * 2
            if m >= 2:
                P.wait("tensor", "ACT", act_ev[m - 2])
            for kc in range(FC):
                if m == 0:
                    P.wait("tensor", "DVE", xn_base + kc + 1)
                for tg in range(2):
                    mm(P, ps[bsel + tg][:, :],
                       latbuf[:, slot * 2048 + kc * 128: slot * 2048 + (kc + 1) * 128],
                       xn_t[:, kc * TOK + tg * 512: kc * TOK + tg * 512 + 512],
                       start=(kc == 0), stop=(kc == FC - 1),
                       inc=("PE", 1) if (kc == FC - 1 and tg == 1) else None)
            consumed[m] = P.n["PE"]
            if m + 2 < MCH:
                P.wait("gpsimd", "PE", consumed[m])
                load_l(m + 2)
            P.wait("scalar", "PE", consumed[m])
            for tg in range(2):
                acopy(P, dest_t[:, dest_off(m, tg):dest_off(m, tg) + 512],
                      ps[bsel + tg][:, :], inc=("ACT", 1) if tg == 1 else None)
            act_ev[m] = P.n["ACT"]
        return P.n["ACT"]

    def finish(src_t):
        P.barrier(engs=("sync",))
        dma(P, "sync", d_y.rearrange("a p n -> p a n"),
            src_t[:, :].rearrange("p (a n) -> p a n", a=FC), "ST")
        P.wait("sync", "ST", P.n["ST"])

    preE = emit_ffn_preload(gbuf, ubuf, "write")
    b1 = emit_ln(d_xe, None, xn, abc, bbc, "hbm")
    resE, peE = emit_ffn(xn, hdn, gbuf, ubuf, dbuf, [0, 1], d_xe, "stream", b1,
                         wc="write", pre=preE)
    if upto == "ffnE":
        finish(resid)
        return _finalize(nc, P, root)
    preLE = emit_lat_preload(d_wkvd, KC)
    b2 = emit_ln(None, resid, xn2, abc2, bbc2, "sbuf",
                 ready={fc: ("DVE", resE[fc]) for fc in range(FC)})
    actkvE = emit_lat(xn2, b2, d_wkvd, KC, kvlat,
                      lambda m, tg: m * 2048 + 512 + tg * 512, pre=preLE)

    preO = emit_ffn_preload(gbuf, ubuf, "read", pe_guard=peE, dve_guard=b2 + FC)
    b3 = emit_ln(d_xo, None, xn, abc, bbc, "hbm")
    resO, peO = emit_ffn(xn, hdn, gbuf, ubuf, dbuf, [0, 1], d_xo, "stream", b3,
                         wc="read", pre=preO)
    if upto == "ffnO":
        finish(resid)
        return _finalize(nc, P, root)
    preLO = emit_lat_preload(d_wkvd, KC)
    b4 = emit_ln(None, resid, xn2, abc2, bbc2, "sbuf",
                 ready={fc: ("DVE", resO[fc]) for fc in range(FC)})
    actkvO = emit_lat(xn2, b4, d_wkvd, KC, kvlat,
                      lambda m, tg: m * 2048 + (0 if tg == 0 else 1536), pre=preLO)
    actql = emit_lat(xn2, b4, d_wqd, QC, qlat,
                     lambda m, tg: m * TOK + tg * 512)
    if dbg:
        P.wait("sync", "ACT", actql)
        dma(P, "sync", d_dkv.rearrange("a p n -> p a n"),
            kvlat[:, :].rearrange("p (a n) -> p a n", a=KC), "ST")
        dma(P, "sync", d_dql.rearrange("a p n -> p a n"),
            qlat[:, :].rearrange("p (a n) -> p a n", a=QC), "ST")
    if upto == "lat":
        finish(resid)
        return _finalize(nc, P, root)

    P.barrier(engs=("sync", "gpsimd"))
    kt_v = dma(P, "sync", ktab[:, :], d_ktab, "XC")
    qt_v = dma(P, "sync", qtab[:, :], d_qtab, "XC")

    order = [(g, h) for g in range(NG) for h in range(H)]
    aw_loaded = {}

    def load_aw(oi):
        buf = awb0 if oi % 2 == 0 else awb1
        aw_loaded[oi] = dma(P, "gpsimd", buf[:, :], d_aw[order[oi][1]], f"A{oi % 2}")

    load_aw(0)
    load_aw(1)

    tri_v = None
    pending_tail = None
    dve_evac = {"v": 0}
    wo_res_final = {}

    def emit_tail(pt):
        th, trec = pt
        P.wait("tensor", "DVE", trec)
        mm(P, ps[7][:, :], ones_f[0:1, 0:128], irow[:, :], start=True, stop=True,
           inc=("PE", 1))
        P.wait("vector", "PE", P.n["PE"])
        vop(P, "tensor_copy", invlb[:, :], ps[7][:, :], inc=("DVE", 1))
        P.wait("vector", "DVE", P.n["DVE"])
        r0 = th * 192
        t0, o0 = divmod(r0, 128)
        if o0 == 0:
            vop(P, "tensor_mul", aog[:, t0 * 512:(t0 + 1) * 512], ps[2][:, :],
                invlb[:, :])
            vop(P, "tensor_mul", aog[0:64, (t0 + 1) * 512:(t0 + 2) * 512],
                ps[3][0:64, :], invlb[0:64, :], inc=("DVE", 1))
        else:
            vop(P, "tensor_mul", aog[64:128, t0 * 512:(t0 + 1) * 512],
                ps[2][0:64, :], invlb[0:64, :])
            vop(P, "tensor_mul", aog[0:64, (t0 + 1) * 512:(t0 + 2) * 512],
                ps[2][64:128, :], invlb[64:128, :])
            vop(P, "tensor_mul", aog[64:128, (t0 + 1) * 512:(t0 + 2) * 512],
                ps[3][0:64, :], invlb[0:64, :], inc=("DVE", 1))
        dve_evac["v"] = P.n["DVE"]


    bank45_guard = [("ACT", 0), ("ACT", 0)]
    vproj_guard = {}
    pe_attn_hist = {-1: 0, -2: 0}
    dve_q_hist = {-1: 0}
    act_q_hist = {-1: 0}

    def vproj_chunks(oi2):
        g2, h2 = order[oi2]
        awb2 = awb0 if oi2 % 2 == 0 else awb1
        vbuf = vh if oi2 % 2 == 0 else vh2
        NTBP2 = EXT[g2] // 2
        guards = []
        AWV2 = 3072
        for tbp in range(NTBP2):
            bank = 4 + (tbp % 2)
            if tbp == 0:
                P.wait("tensor", f"A{oi2 % 2}", aw_loaded[oi2])
            if tbp < 2:
                P.wait("tensor", *bank45_guard[tbp % 2])
                if bank == 5:
                    P.wait("tensor", "ACT", act_q_hist[oi2 - 1])
            else:
                P.wait("tensor", *guards[tbp - 2])
            for sub in range(2):
                tb = tbp * 2 + sub
                for kc in range(KC):
                    mm(P, ps[bank][:, sub * 192:sub * 192 + 192],
                       kvlat[:, kc * 2048 + tb * 128: kc * 2048 + (tb + 1) * 128],
                       awb2[:, AWV2 + kc * 192: AWV2 + (kc + 1) * 192],
                       start=(kc == 0), stop=(kc == KC - 1),
                       inc=("PE", 1) if (kc == KC - 1 and sub == 1) else None)
            pvv = P.n["PE"]
            P.wait("scalar", "PE", pvv)
            P.wait("scalar", "PE", pe_attn_hist[oi2 - 2])
            acopy(P, vbuf[:, (tbp * 2) * VST:(tbp * 2) * VST + 192],
                  ps[bank][:, 0:192])
            acopy(P, vbuf[:, (tbp * 2 + 1) * VST:(tbp * 2 + 1) * VST + 192],
                  ps[bank][:, 192:384], inc=("ACT", 1))
            guards.append(("ACT", P.n["ACT"]))
            bank45_guard[tbp % 2] = guards[-1]
            vproj_guard[oi2] = list(guards)
            yield

    woven = {}

    for oi, (g, h) in enumerate(order):
        awb = awb0 if oi % 2 == 0 else awb1
        AWQ, AWK, AWV = 0, 2304, 3072
        if h == 0:
            P.wait("sync", "DVE", P.n["DVE"])
            tri_v = dma(P, "sync", tri_g[:, :], d_tri[g], "XC")
            if oi > 0:
                P.wait("gpsimd", "PE", P.n["PE"])
                load_aw(oi)
                load_aw(oi + 1)
            P.wait("vector", "PE", P.n["PE"])
            for tb in range(NKB):
                vop(P, "memset", vh[:, tb * VST + 192: tb * VST + 193], 1.0)
                vop(P, "memset", vh2[:, tb * VST + 192: tb * VST + 193], 1.0,
                    inc=("DVE", 1) if tb == NKB - 1 else None)
            P.wait("tensor", "DVE", P.n["DVE"])
            P.wait("vector", "XC", tri_v)
            P.wait("tensor", "ACT", max(actkvO, actkvE, actql))

        NGRP = EXT[g] // 4
        vbuf_self = vh if oi % 2 == 0 else vh2

        if not woven.get(oi):
            for _ in vproj_chunks(oi):
                pass

        act_head0 = P.n["ACT"]
        dve_head0 = P.n["DVE"]
        act_k_ev = {}
        dve_k = {}
        for ng in range(NGRP):
            bn = 5 if ng % 2 == 0 else 0
            br = 6 if ng % 2 == 0 else 1
            if ng % 2 == 0:
                if ng == 0:
                    for g_ in vproj_guard[oi]:
                        P.wait("tensor", *g_)
                    P.wait("tensor", *bank45_guard[1])
                    P.wait("tensor", "DVE", dve_q_hist[oi - 1])
                else:
                    P.wait("tensor", "ACT", act_k_ev[ng - 2])
                    P.wait("tensor", "DVE", dve_k[ng - 2])
            else:
                if ng == 1:
                    P.wait("tensor", "ACT", act_head0)
                    P.wait("tensor", "DVE", dve_head0)
                else:
                    P.wait("tensor", "ACT", act_k_ev[ng - 2])
                    P.wait("tensor", "DVE", dve_k[ng - 2])
            for kc in range(KC):
                mm(P, ps[bn][:, :],
                   awb[:, AWK + kc * 192: AWK + kc * 192 + 128],
                   kvlat[:, kc * 2048 + ng * 512: kc * 2048 + (ng + 1) * 512],
                   start=(kc == 0), stop=(kc == KC - 1))
                mm(P, ps[br][0:64, :],
                   awb[:, AWK + kc * 192 + 128: AWK + (kc + 1) * 192],
                   kvlat[:, kc * 2048 + ng * 512: kc * 2048 + (ng + 1) * 512],
                   start=(kc == 0), stop=(kc == KC - 1),
                   inc=("PE", 1) if kc == KC - 1 else None)
            pk = P.n["PE"]
            P.wait("scalar", "PE", pk)
            acopy(P, kh_a[:, ng * 512:(ng + 1) * 512], ps[bn][:, :], inc=("ACT", 1))
            act_k_ev[ng] = P.n["ACT"]
            P.wait("vector", "PE", pk)
            if ng >= 1:
                P.wait("vector", "DVE", dve_k[ng - 1])
            vop(P, "tensor_mul", ropeA[:, :], ps[br][0:64, :],
                ktab[0:64, ng * 512:(ng + 1) * 512])
            vop(P, "tensor_mul", ropeB[0:32, :], ps[br][32:64, :],
                ktab[64:96, ng * 512:(ng + 1) * 512])
            vop(P, "tensor_mul", ropeB[32:64, :], ps[br][0:32, :],
                ktab[64:96, ng * 512:(ng + 1) * 512], inc=("DVE", 1))
            P.wait("vector", "DVE", P.n["DVE"])
            vop(P, "tensor_sub", kh_b[0:32, ng * 512:(ng + 1) * 512],
                ropeA[0:32, :], ropeB[0:32, :])
            vop(P, "tensor_add", kh_b[32:64, ng * 512:(ng + 1) * 512],
                ropeA[32:64, :], ropeB[32:64, :], inc=("DVE", 1))
            dve_k[ng] = P.n["DVE"]

        P.wait("tensor", "ACT", act_k_ev[max(0, NGRP - 2)])
        P.wait("tensor", "DVE", dve_k[max(0, NGRP - 2)])
        for kc in range(QC):
            mm(P, ps[5][:, :], awb[:, AWQ + kc * 192: AWQ + kc * 192 + 128],
               qlat[:, kc * TOK + g * 512: kc * TOK + (g + 1) * 512],
               start=(kc == 0), stop=(kc == QC - 1))
            mm(P, ps[6][0:64, :], awb[:, AWQ + kc * 192 + 128: AWQ + (kc + 1) * 192],
               qlat[:, kc * TOK + g * 512: kc * TOK + (g + 1) * 512],
               start=(kc == 0), stop=(kc == QC - 1),
               inc=("PE", 1) if kc == QC - 1 else None)
        pq = P.n["PE"]
        P.wait("scalar", "PE", pq)
        acopy(P, qh_a[:, :], ps[5][:, :], inc=("ACT", 1))
        act_q_ev = P.n["ACT"]
        act_q_hist[oi] = act_q_ev
        P.wait("vector", "PE", pq)
        P.wait("vector", "DVE", dve_k[NGRP - 1])
        vop(P, "tensor_mul", ropeA[:, :], ps[6][0:64, :], qtab[0:64, g * 512:(g + 1) * 512])
        vop(P, "tensor_mul", ropeB[0:32, :], ps[6][32:64, :], qtab[64:96, g * 512:(g + 1) * 512])
        vop(P, "tensor_mul", ropeB[32:64, :], ps[6][0:32, :], qtab[64:96, g * 512:(g + 1) * 512],
            inc=("DVE", 1))
        P.wait("vector", "DVE", P.n["DVE"])
        vop(P, "tensor_sub", qh_b[0:32, :], ropeA[0:32, :], ropeB[0:32, :])
        vop(P, "tensor_add", qh_b[32:64, :], ropeA[32:64, :], ropeB[32:64, :],
            inc=("DVE", 1))
        dve_q = P.n["DVE"]
        dve_q_hist[oi] = dve_q

        if pending_tail is not None:
            emit_tail(pending_tail)
            pending_tail = None

        exp_done = {}
        pdone = {}

        def emit_scores(kb):
            sb = kb % 2
            if kb == 0:
                P.wait("tensor", "ACT", act_q_ev)
            if kb >= 2:
                P.wait("tensor", "ACT", exp_done[kb - 2])
            mm(P, ps[sb][:, :], kh_a[:, kb * 128:(kb + 1) * 128], qh_a[:, :],
               start=True, stop=False)
            if kb == 0:
                P.wait("tensor", "DVE", dve_q)
            mm(P, ps[sb][:, :], kh_b[:, kb * 128:(kb + 1) * 128], qh_b[:, :],
               start=False, stop=True, inc=("PE", 1))
            psc = P.n["PE"]
            P.wait("scalar", "PE", psc)
            pt = ptile[:, (kb % 3) * 512:(kb % 3) * 512 + 512]
            act(P, pt, ps[sb][:, :], AF.Exp,
                bias=t_badd[:, g * 16 + kb: g * 16 + kb + 1], scale=SCALE,
                inc=("ACT", 1))
            exp_done[kb] = P.n["ACT"]
            if DIAG[g] <= kb < DIAG[g] + 4:
                jj = kb - DIAG[g]
                P.wait("vector", "ACT", exp_done[kb])
                vop(P, "tensor_mul", pt, pt, tri_g[:, jj * 512:(jj + 1) * 512],
                    inc=("DVE", 1))
                pdone[kb] = ("DVE", P.n["DVE"])
            else:
                pdone[kb] = ("ACT", exp_done[kb])

        def emit_pv(kb):
            st, v = pdone[kb]
            P.wait("tensor", st, v)
            if kb == 0:
                P.wait("tensor", "DVE", dve_evac["v"])
                for g_ in vproj_guard[oi]:
                    P.wait("tensor", *g_)
            pt = ptile[:, (kb % 3) * 512:(kb % 3) * 512 + 512]
            mm(P, ps[2][:, :], vbuf_self[:, kb * VST: kb * VST + 128], pt,
               start=(kb == 0), stop=(kb == EXT[g] - 1))
            mm(P, ps[3][0:65, :], vbuf_self[:, kb * VST + 128: kb * VST + 193], pt,
               start=(kb == 0), stop=(kb == EXT[g] - 1),
               inc=("PE", 1) if kb == EXT[g] - 1 else None)

        weave = None
        if oi + 1 < len(order) and (oi + 1) % H != 0:
            weave = vproj_chunks(oi + 1)
            woven[oi + 1] = True

        def step_weave():
            if weave is not None:
                next(weave, None)

        for _ in range(4):
            step_weave()

        emit_scores(0)
        emit_scores(1)
        for kb in range(2, EXT[g]):
            emit_scores(kb)
            step_weave()
            emit_pv(kb - 2)
        step_weave()
        emit_pv(EXT[g] - 2)
        step_weave()
        emit_pv(EXT[g] - 1)
        if weave is not None:
            for _ in weave:
                pass
        pe_attn = P.n["PE"]
        pe_attn_hist[oi] = pe_attn

        if oi + 2 < len(order) and (oi + 2) % H not in (0, 1):
            P.wait("gpsimd", "PE", pe_attn)
            load_aw(oi + 2)

        P.wait("vector", "PE", pe_attn)
        vop(P, "tensor_copy", lrow[:, :], ps[3][64:65, :], inc=("DVE", 1))
        P.wait("vector", "DVE", P.n["DVE"])
        vop(P, "reciprocal", irow[:, :], lrow[:, :], inc=("DVE", 1))
        pending_tail = (h, P.n["DVE"])
        if h == H - 1:
            emit_tail(pending_tail)
            pending_tail = None

        if h == H - 1:
            dve_aog = P.n["DVE"]
            wo_res = {}
            P.wait("gpsimd", "PE", P.n["PE"])
            P.wait("gpsimd", "DVE", P.n["DVE"])
            wo_v = {}
            wo_pe = {}

            wo_bufs = (wob, wob1, wob2)
            wo_sems = ("WO", "W1", "W2")

            def load_wo(mc):
                wo_v[mc] = dma(P, "gpsimd", wo_bufs[mc % 3][:, :], d_wo[mc],
                               wo_sems[mc % 3])

            load_wo(0)
            load_wo(1)
            load_wo(2)
            for mc in range(FC):
                wbuf = wo_bufs[mc % 3]
                P.wait("tensor", wo_sems[mc % 3], wo_v[mc])
                if mc == 0:
                    P.wait("tensor", "DVE", dve_aog)
                bank = mc % 4
                if mc >= 4:
                    P.wait("tensor", "DVE", wo_res[mc - 4])
                for kc in range(24):
                    mm(P, ps[bank][:, :], wbuf[:, kc * 128:(kc + 1) * 128],
                       aog[:, kc * 512:(kc + 1) * 512],
                       start=(kc == 0), stop=(kc == 23),
                       inc=("PE", 1) if kc == 23 else None)
                pwo = P.n["PE"]
                wo_pe[mc] = pwo
                if mc + 3 < FC:
                    P.wait("gpsimd", "PE", pwo)
                    load_wo(mc + 3)
                P.wait("vector", "PE", pwo)
                dst = resid[:, mc * TOK + g * 512: mc * TOK + g * 512 + 512]
                vop(P, "tensor_add", dst, ps[bank][:, :], dst, inc=("DVE", 1))
                wo_res[mc] = P.n["DVE"]
                if g == NG - 1:
                    wo_res_final[mc] = P.n["DVE"]

    if upto == "attn":
        finish(resid)
        return _finalize(nc, P, root)

    preF = emit_ffn_preload(gbuf2, ubuf2, "read")
    b5 = emit_ln(None, resid, xn3, abc2, bbc2, "sbuf",
                 ready={fc: ("DVE", wo_res_final[fc]) for fc in range(FC)})
    emit_ffn(xn3, hdn2, gbuf2, ubuf2, dbuf2, [0], None, "inplace", b5, wc="read",
             pre=preF)
    emit_ffn(xn3, hdn2, gbuf2, ubuf2, dbuf2, [1], None, "inplace", b5, wc="read")
    del preF

    finish(resid)
    return _finalize(nc, P, root)


def _finalize(nc, P, root):
    with nc.Block() as block:
        @block.sync
        def _(e):
            for f in P.ops["sync"]:
                f(e)

        @block.gpsimd
        def _(e):
            for f in P.ops["gpsimd"]:
                f(e)

        @block.tensor
        def _(e):
            for f in P.ops["tensor"]:
                f(e)

        @block.vector
        def _(e):
            for f in P.ops["vector"]:
                f(e)

        @block.scalar
        def _(e):
            for f in P.ops["scalar"]:
                f(e)

    root.close()
    return nc



def _prep_shared(inp):
    f32 = lambda a: np.ascontiguousarray(np.asarray(a, dtype=np.float32))
    gate, up, down = f32(inp["L0_ffn_gate_w"]), f32(inp["L0_ffn_up_w"]), f32(inp["L0_ffn_down_w"])
    gw = gate.T.reshape(FC, 128, IC, 128).transpose(2, 1, 0, 3).reshape(IC, 128, FC * 128)
    uw = up.T.reshape(FC, 128, IC, 128).transpose(2, 1, 0, 3).reshape(IC, 128, FC * 128)
    dw = down.T.reshape(IC, 128, FC, 128).transpose(2, 1, 0, 3).reshape(FC, 128, IC * 128)
    gsc = f32(inp["L0_ffn_gate_s"]).reshape(IC, 128).T
    usc = f32(inp["L0_ffn_up_s"]).reshape(IC, 128).T
    dsc = f32(inp["L0_ffn_down_s"]).reshape(FC, 128).T
    wqd = f32(inp["L0_attn_wq_down"]).reshape(FC, 128, QC, 128).transpose(2, 1, 0, 3).reshape(QC, 128, FC * 128)
    wkvd = f32(inp["L0_attn_wkv_down"]).reshape(FC, 128, KC, 128).transpose(2, 1, 0, 3).reshape(KC, 128, FC * 128)
    wo = f32(inp["L0_attn_wo"]).reshape(24, 128, FC, 128).transpose(2, 1, 0, 3).reshape(FC, 128, 24 * 128)
    wqn, wqr = f32(inp["L0_attn_wq_nope"]), f32(inp["L0_attn_wq_rope"])
    wkn, wkr = f32(inp["L0_attn_wk_nope"]), f32(inp["L0_attn_wk_rope"])
    wvu = f32(inp["L0_attn_wv_up"])
    perm = np.concatenate([np.arange(0, 64, 2), np.arange(1, 64, 2)])
    aw = np.empty((H, 128, 3840), dtype=np.float32)
    for h in range(H):
        q = np.concatenate([wqn[:, h * 128:(h + 1) * 128],
                            wqr[:, h * 64:(h + 1) * 64][:, perm]], axis=1)
        k = np.concatenate([wkn[:, h * 128:(h + 1) * 128],
                            wkr[:, h * 64:(h + 1) * 64][:, perm]], axis=1)
        v = wvu[:, h * 192:(h + 1) * 192]
        aw[h, :, 0:2304] = q.reshape(QC, 128, 192).transpose(1, 0, 2).reshape(128, 2304)
        aw[h, :, 2304:3072] = k.reshape(KC, 128, 192).transpose(1, 0, 2).reshape(128, 768)
        aw[h, :, 3072:3840] = v.reshape(KC, 128, 192).transpose(1, 0, 2).reshape(128, 768)
    return dict(gw=gw, uw=uw, dw=dw, gsc=np.ascontiguousarray(gsc),
                usc=np.ascontiguousarray(usc), dsc=np.ascontiguousarray(dsc),
                wqd=wqd, wkvd=wkvd, wo=wo, aw=aw)


def _rope_tab(pos, width):
    inv = 1.0 / (10000.0 ** (np.arange(0, DR, 2, dtype=np.float64) / DR))
    ang = np.outer(pos.astype(np.float64), inv)
    cos = np.cos(ang).T.astype(np.float32)
    sin = np.sin(ang).T.astype(np.float32)
    tab = np.zeros((128, width), dtype=ml_dtypes.bfloat16)
    n = pos.shape[0]
    tab[0:32, :n] = cos
    tab[32:64, :n] = cos
    tab[64:96, :n] = sin
    return tab


def _core_layout(s):
    q = [np.arange(i * 512, (i + 1) * 512) for i in range(4)]
    if s == 0:
        pos_q = np.concatenate([q[0], q[3]])
        pos_e = np.concatenate([q[1], q[2]])
    else:
        pos_q = np.concatenate([q[1], q[2]])
        pos_e = np.concatenate([q[0], q[3]])
    pos_k = np.concatenate([pos_q[0:512], pos_e, pos_q[512:1024]])
    return pos_q, pos_e, pos_k


def _prep_core(inp, core):
    b, s = core // 2, core % 2
    x = np.asarray(inp["L0_attn_x"], dtype=np.float32)
    pos_q, pos_e, pos_k = _core_layout(s)
    own = np.ascontiguousarray(x[b][pos_q])
    extra = np.ascontiguousarray(x[b][pos_e])
    xo = np.ascontiguousarray(own.T.reshape(FC, 128, TOK))
    xe = np.ascontiguousarray(extra.T.reshape(FC, 128, TOK))
    ktab = _rope_tab(pos_k, 2048)
    qtab = _rope_tab(pos_q, TOK)
    tri = np.zeros((NG, 128, 4 * TGS), dtype=ml_dtypes.bfloat16)
    badd = np.zeros((128, 32), dtype=np.float32)
    for g in range(NG):
        qp = pos_q[g * TGS:(g + 1) * TGS]
        for kb in range(EXT[g]):
            kp = pos_k[kb * 128:(kb + 1) * 128]
            vis = kp[:, None] <= qp[None, :]
            if DIAG[g] <= kb < DIAG[g] + 4:
                jj = kb - DIAG[g]
                tri[g, :, jj * TGS:(jj + 1) * TGS] = vis.astype(ml_dtypes.bfloat16)
                badd[:, g * 16 + kb] = -MSHIFT
            elif not vis.any():
                badd[:, g * 16 + kb] = NEGBIG
            else:
                assert vis.all(), f"core{core} g{g} kb{kb}: partial mask outside diag"
                badd[:, g * 16 + kb] = -MSHIFT
    return dict(xo=xo, xe=xe, ktab=ktab, qtab=qtab, tri=tri, badd=badd)


def _assemble(results):
    out = np.empty((B, T, C), dtype=np.float32)
    for core in range(8):
        b, s = core // 2, core % 2
        pos_q, _, _ = _core_layout(s)
        y = np.asarray(results[core]["y"], dtype=np.float32)
        out[b, pos_q] = y.reshape(C, TOK).T
    return out


_NC_CACHE = {}


def kernel(**inputs):
    shared = _prep_shared(inputs)
    in_maps = []
    for core in range(8):
        m = dict(shared)
        m.update(_prep_core(inputs, core))
        in_maps.append(m)
    if "prog" not in _NC_CACHE:
        _NC_CACHE["prog"] = build_program()
    res = run_bass_kernel_spmd(_NC_CACHE["prog"], in_maps, core_ids=list(range(8)))
    return _assemble(res.results)


# version 22
# speedup vs baseline: 1.0682x; 1.0682x over previous
import numpy as np
import ml_dtypes
from contextlib import ExitStack

import concourse.bass as bass
import concourse.mybir as mybir
from concourse.bass_utils import run_bass_kernel_spmd

F32 = mybir.dt.float32
BF16 = mybir.dt.bfloat16
AF = mybir.ActivationFunctionType
ALU = mybir.AluOpType
AX = mybir.AxisListType

B, T, C = 4, 2048, 2048
FC = 16
INTER, IC = 5632, 44
QR, QC = 1536, 12
KVR, KC = 512, 4
H, DN, DR, D = 16, 128, 64, 192
TOK = 1024
TGS = 512
NG = 2
NKB = 16
SCALE = float(1.0 / np.sqrt(192.0))
EPS = 1e-5
NEGBIG = -1.0e30
MARGIN = 1.0625
MSHIFT = 40.0
EXT = {0: 8, 1: 16}
DIAG = {0: 0, 1: 12}

ENGINES = ("sync", "gpsimd", "tensor", "vector", "scalar")
STREAMS = ("XC", "X0", "X1", "ST", "G0", "G1", "D0", "D1", "L0", "L1",
           "A0", "A1", "WO", "W1", "W2", "CW0", "CW1", "CD0", "CD1", "RG0", "RG1", "RD0", "RD1", "PE", "ACT", "DVE")


class Prog:
    def __init__(self):
        self.ops = {e: [] for e in ENGINES}
        self.n = {s: 0 for s in STREAMS}
        self.hw = {e: {s: 0 for s in STREAMS} for e in ENGINES}
        self.sems = {}

    def emit(self, eng, fn):
        self.ops[eng].append(fn)

    def wait(self, eng, stream, val):
        if val <= 0 or val <= self.hw[eng][stream]:
            return
        self.hw[eng][stream] = val
        self.emit(eng, lambda e, s=stream, v=val: e.wait_ge(self.sems[s], v))

    def barrier(self, engs=ENGINES, streams=("PE", "ACT", "DVE")):
        for eng in engs:
            for s in streams:
                self.wait(eng, s, self.n[s])

    def bump(self, stream, k=1):
        self.n[stream] += k
        return self.n[stream]


def _bump_inc(P, inc):
    if inc is None:
        return None
    P.bump(inc[0], inc[1])
    return (inc[0], inc[1])


def _attach(ins, P, inc):
    if inc is not None:
        ins.then_inc(P.sems[inc[0]], inc[1])


def mm(P, out, lhsT, rhs, start, stop, inc=None):
    inc = _bump_inc(P, inc)

    def f(e, out=out, l=lhsT, r=rhs, st=start, sp=stop, inc=inc, P=P):
        _attach(e.matmul(out, l, r, start=st, stop=sp), P, inc)
    P.emit("tensor", f)


def act(P, out, in_, func, bias=0.0, scale=1.0, inc=None):
    inc = _bump_inc(P, inc)

    def f(e, out=out, i=in_, fu=func, b=bias, s=scale, inc=inc, P=P):
        _attach(e.activation(out, i, fu, bias=b, scale=s), P, inc)
    P.emit("scalar", f)


def acopy(P, out, in_, inc=None):
    inc = _bump_inc(P, inc)

    def f(e, out=out, i=in_, inc=inc, P=P):
        _attach(e.copy(out, i), P, inc)
    P.emit("scalar", f)


def vop(P, name, *args, inc=None, **kw):
    inc = _bump_inc(P, inc)

    def f(e, n=name, a=args, kw=kw, inc=inc, P=P):
        _attach(getattr(e, n)(*a, **kw), P, inc)
    P.emit("vector", f)


def dma(P, eng, out, in_, stream):
    v = P.bump(stream, 16)

    def f(e, out=out, i=in_, P=P, s=stream):
        e.dma_start(out=out, in_=i).then_inc(P.sems[s], 16)
    P.emit(eng, f)
    return v


def build_program(upto="all", dbg=False, race_detect=True):
    nc = bass.Bass("TRN2", target_bir_lowering=False,
                   detect_race_conditions=race_detect)
    P = Prog()

    d_xo = nc.dram_tensor("xo", [FC, 128, TOK], F32, kind="ExternalInput").ap()
    d_xe = nc.dram_tensor("xe", [FC, 128, TOK], F32, kind="ExternalInput").ap()
    d_gw = nc.dram_tensor("gw", [IC, 128, FC * 128], F32, kind="ExternalInput").ap()
    d_uw = nc.dram_tensor("uw", [IC, 128, FC * 128], F32, kind="ExternalInput").ap()
    d_dw = nc.dram_tensor("dw", [FC, 128, IC * 128], F32, kind="ExternalInput").ap()
    d_gs = nc.dram_tensor("gsc", [128, IC], F32, kind="ExternalInput").ap()
    d_us = nc.dram_tensor("usc", [128, IC], F32, kind="ExternalInput").ap()
    d_ds = nc.dram_tensor("dsc", [128, FC], F32, kind="ExternalInput").ap()
    d_wqd = nc.dram_tensor("wqd", [QC, 128, FC * 128], F32, kind="ExternalInput").ap()
    d_wkvd = nc.dram_tensor("wkvd", [KC, 128, FC * 128], F32, kind="ExternalInput").ap()
    d_aw = nc.dram_tensor("aw", [H, 128, 3840], F32, kind="ExternalInput").ap()
    d_wo = nc.dram_tensor("wo", [FC, 128, 3072], F32, kind="ExternalInput").ap()
    d_ktab = nc.dram_tensor("ktab", [128, 2048], BF16, kind="ExternalInput").ap()
    d_qtab = nc.dram_tensor("qtab", [128, TOK], BF16, kind="ExternalInput").ap()
    d_tri = nc.dram_tensor("tri", [NG, 128, 4 * TGS], BF16, kind="ExternalInput").ap()
    d_badd = nc.dram_tensor("badd", [128, 32], F32, kind="ExternalInput").ap()
    d_y = nc.dram_tensor("y", [FC, 128, TOK], F32, kind="ExternalOutput").ap()
    c_gw = nc.dram_tensor("c_gw", [IC, 128, FC * 128], BF16, kind="Internal").ap()
    c_uw = nc.dram_tensor("c_uw", [IC, 128, FC * 128], BF16, kind="Internal").ap()
    c_dw = nc.dram_tensor("c_dw", [FC, 128, IC * 128], BF16, kind="Internal").ap()
    if dbg:
        d_dkv = nc.dram_tensor("dbg_kv", [KC, 128, 2048], BF16, kind="ExternalOutput").ap()
        d_dql = nc.dram_tensor("dbg_ql", [QC, 128, TOK], BF16, kind="ExternalOutput").ap()
        d_dat = nc.dram_tensor("dbg_at", [128, 8192], F32, kind="ExternalOutput").ap()
        d_dao = nc.dram_tensor("dbg_ao", [128, 12288], BF16, kind="ExternalOutput").ap()

    B0 = 16512
    assert B0 >= nc.sbuf_base
    TOPB = nc.sbuf_top

    def S(name, shape, dtype, off):
        assert off % 32 == 0, name
        nbytes = shape[1] * (4 if dtype == F32 else 2)
        assert B0 <= off and off + nbytes <= TOPB, (name, off, nbytes, off + nbytes)
        return nc.alloc_sbuf_tensor_at(name, shape, dtype, offset=off)

    ones_bf = S("ones_bf", [128, 128], BF16, B0 + 0)
    ones_f = S("ones_f", [128, 128], F32, B0 + 256)
    t_gs = S("t_gs", [128, IC], F32, B0 + 768)
    t_us = S("t_us", [128, IC], F32, B0 + 960)
    t_ds = S("t_ds", [128, FC], F32, B0 + 1152)
    t_hds = S("t_hds", [128, FC], F32, B0 + 1216)
    t_badd = S("t_badd", [128, 32], F32, B0 + 1280)
    scr0 = S("scr0", [128, 512], F32, B0 + 1408)
    scr1 = S("scr1", [128, 512], F32, B0 + 3456)
    scrb0 = S("scrb0", [128, 1024], BF16, B0 + 1408)
    scrb1 = S("scrb1", [128, 1024], BF16, B0 + 3456)
    scrb0b = S("scrb0b", [128, 1024], BF16, B0 + 13696)
    scrb1b = S("scrb1b", [128, 1024], BF16, B0 + 15744)
    mrow = S("mrow", [1, 512], F32, B0 + 5504)
    vrow = S("vrow", [1, 512], F32, B0 + 7552)
    brow = S("brow", [1, 512], F32, B0 + 9600)
    trow = S("trow", [1, 512], F32, B0 + 11648)
    lrow = S("lrow", [1, 512], F32, B0 + 13696)
    irow = S("irow", [1, 512], F32, B0 + 15744)
    kvlat = S("kvlat", [128, KC * 2048], BF16, B0 + 17792)

    A0 = B0 + 34176

    xstrm = S("xstrm", [128, 2048], F32, A0 + 0)
    resid = S("resid", [128, FC * TOK], F32, A0 + 8192)
    xn = S("xn", [128, FC * TOK], BF16, A0 + 8192)
    gbuf = S("gbuf", [128, 4096], BF16, A0 + 40960)
    ubuf = S("ubuf", [128, 4096], BF16, A0 + 49152)
    abc = S("abc", [128, TOK], F32, A0 + 57344)
    bbc = S("bbc", [128, TOK], F32, A0 + 61440)
    hdn = S("hdn", [128, IC * TOK], BF16, A0 + 73728)
    dbuf = S("dbuf", [128, IC * 128], BF16, A0 + 163840)
    abc2 = S("abc2", [128, TOK], F32, A0 + 0)
    bbc2 = S("bbc2", [128, TOK], F32, A0 + 4096)

    xn2 = S("xn2", [128, FC * TOK], BF16, A0 + 73728)
    latbuf = S("latbuf", [128, 4096], BF16, A0 + 106496)
    qlat = S("qlat", [128, QC * TOK], BF16, A0 + 114688)

    ktab = S("ktab", [128, 2048], BF16, A0 + 73728)
    qtab = S("qtab", [128, TOK], BF16, A0 + 77824)
    tri_g = S("tri_g", [128, 2048], BF16, A0 + 79872)
    kh_a = S("kh_a", [128, 2048], BF16, A0 + 83968)
    kh_b = S("kh_b", [64, 2048], BF16, A0 + 88064)
    qh_a = S("qh_a", [128, 512], BF16, A0 + 92160)
    qh_b = S("qh_b", [64, 512], BF16, A0 + 93184)
    VST = 194
    vh = S("vh", [128, NKB * VST], BF16, A0 + 94208)
    vh2 = S("vh2", [128, NKB * VST], BF16, A0 + 100416)
    ropeA = S("ropeA", [64, 512], F32, A0 + 106624)
    ropeB = S("ropeB", [64, 512], F32, A0 + 108672)
    invlb = S("invlb", [128, 512], F32, A0 + 110720)
    wob = S("wob", [128, 3072], BF16, A0 + 83968)
    wob1 = S("wob1", [128, 3072], BF16, A0 + 163840)
    wob2 = S("wob2", [128, 3072], BF16, A0 + 94208)
    aog = S("aog", [128, 24 * TGS], BF16, A0 + 139264)
    awb0 = S("awb0", [128, 3840], BF16, A0 + 163840)
    ptile = S("ptile", [128, 3 * 512], BF16, A0 + 171520)
    awb1 = S("awb1", [128, 3840], BF16, A0 + 0)

    xn3 = S("xn3", [128, FC * TOK], BF16, A0 + 73728)
    gbuf2 = S("gbuf2", [128, 4096], BF16, A0 + 106496)
    ubuf2 = S("ubuf2", [128, 4096], BF16, B0 + 17792)
    hdn2 = S("hdn2", [128, IC * TGS], BF16, A0 + 114688)
    dbuf2 = S("dbuf2", [128, IC * 128], BF16, A0 + 159744)

    ps = [nc.alloc_psum_tensor(f"ps{i}", [128, 512], F32) for i in range(8)]
    nc._dbg_tensors = dict(resid=resid, xn=xn, abc=abc, bbc=bbc, hdn=hdn, xn2=xn2,
                           kvlat=kvlat, qlat=qlat, scr0=scr0, scr1=scr1, mrow=mrow,
                           vrow=vrow, brow=brow, xstrm=xstrm, kh_a=kh_a, kh_b=kh_b,
                           qh_a=qh_a, qh_b=qh_b, vh=vh, vh2=vh2, aog=aog,
                           invlb=invlb, xn3=xn3, hdn2=hdn2, ptile=ptile)

    root = ExitStack()
    for s in STREAMS:
        P.sems[s] = root.enter_context(nc.semaphore(s))
    root.enter_context(nc.allow_low_precision("bf16 matmuls allowed by problem tolerance"))

    for dst, dsrc in ((t_gs, d_gs), (t_us, d_us), (t_ds, d_ds), (t_badd, d_badd)):
        dma(P, "sync", dst[:, :], dsrc, "XC")
    xh_consts = P.n["XC"]
    vop(P, "memset", ones_bf[:, :], 1.0)
    vop(P, "memset", ones_f[:, :], 1.0, inc=("DVE", 1))
    P.wait("vector", "XC", xh_consts)
    vop(P, "tensor_scalar_mul", t_hds[:, :], t_ds[:, :], 0.5, inc=("DVE", 1))

    def emit_ln(d_src, src_t, xn_t, a_t, b_t, src_kind, ready=None):
        P.wait("scalar", "XC", xh_consts)
        pe_b = P.n["PE"]
        act_b = P.n["ACT"]
        dve_b = P.n["DVE"]
        dve_entry = P.n["DVE"]
        for fc in range(FC):
            if src_kind == "hbm":
                if fc >= 2:
                    P.wait("sync", "PE", pe_b + fc - 1)
                else:
                    P.wait("sync", "DVE", dve_entry)
                xv = dma(P, "sync", xstrm[:, (fc % 2) * 1024:(fc % 2) * 1024 + 1024],
                         d_src[fc], f"X{fc % 2}")
                P.wait("tensor", f"X{fc % 2}", xv)
                P.wait("scalar", f"X{fc % 2}", xv)
                P.wait("vector", f"X{fc % 2}", xv)
                xc = xstrm[:, (fc % 2) * 1024:(fc % 2) * 1024 + 1024]
            else:
                if ready is not None:
                    st_, v_ = ready[fc]
                    P.wait("scalar", st_, v_)
                    P.wait("vector", st_, v_)
                    P.wait("tensor", st_, v_)
                xc = src_t[:, fc * TOK:(fc + 1) * TOK]
            sq = scrb0 if fc % 2 == 0 else scrb0b
            sx = scrb1 if fc % 2 == 0 else scrb1b
            if fc >= 2:
                P.wait("scalar", "PE", pe_b + fc - 1)
                P.wait("vector", "PE", pe_b + fc - 1)
            act(P, sq[:, 0:512], xc[:, 0:512], AF.Square)
            act(P, sq[:, 512:1024], xc[:, 512:1024], AF.Square, inc=("ACT", 1))
            vop(P, "tensor_copy", sx[:, :], xc[:, :], inc=("DVE", 1))
            P.wait("tensor", "ACT", act_b + fc + 1)
            P.wait("tensor", "DVE", dve_b + fc + 1)
            for tg in range(2):
                mm(P, ps[4 + tg][0:1, :], ones_bf[:, 0:1],
                   sx[:, tg * 512:(tg + 1) * 512],
                   start=(fc == 0), stop=(fc == FC - 1))
                mm(P, ps[6 + tg][0:1, :], ones_bf[:, 0:1],
                   sq[:, tg * 512:(tg + 1) * 512],
                   start=(fc == 0), stop=(fc == FC - 1),
                   inc=("PE", 1) if tg == 1 else None)
        pe_stats = P.n["PE"]
        for tg in range(2):
            P.wait("vector", "PE", pe_stats)
            vop(P, "tensor_scalar_mul", mrow[:, :], ps[4 + tg][0:1, :], 1.0 / C)
            vop(P, "tensor_scalar", vrow[:, :], ps[6 + tg][0:1, :], 1.0 / C, EPS,
                ALU.mult, ALU.add)
            vop(P, "tensor_mul", trow[:, :], mrow[:, :], mrow[:, :])
            vop(P, "tensor_sub", vrow[:, :], vrow[:, :], trow[:, :], inc=("DVE", 1))
            P.wait("scalar", "DVE", P.n["DVE"])
            act(P, trow[:, :], vrow[:, :], AF.Ln)
            act(P, vrow[:, :], trow[:, :], AF.Exp, scale=-0.5, inc=("ACT", 1))
            P.wait("vector", "ACT", P.n["ACT"])
            vop(P, "scalar_tensor_tensor", brow[:, :], mrow[:, :], -1.0, vrow[:, :],
                ALU.mult, ALU.mult, inc=("DVE", 1))
            P.wait("tensor", "DVE", P.n["DVE"])
            mm(P, ps[0 + tg][:, :], ones_f[0:1, 0:128], vrow[:, :], start=True, stop=True)
            mm(P, ps[2 + tg][:, :], ones_f[0:1, 0:128], brow[:, :], start=True, stop=True,
               inc=("PE", 1))
            P.wait("vector", "PE", P.n["PE"])
            vop(P, "tensor_copy", a_t[:, tg * 512:(tg + 1) * 512], ps[0 + tg][:, :])
            vop(P, "tensor_copy", b_t[:, tg * 512:(tg + 1) * 512], ps[2 + tg][:, :],
                inc=("DVE", 1))
        dve_ab = P.n["DVE"]
        for fc in range(FC):
            if src_kind == "hbm":
                if fc < 2:
                    P.wait("sync", "PE", pe_stats)
                else:
                    P.wait("sync", "DVE", dve_ab + fc - 1)
                xv = dma(P, "sync", xstrm[:, (fc % 2) * 1024:(fc % 2) * 1024 + 1024],
                         d_src[fc], f"X{fc % 2}")
                P.wait("vector", f"X{fc % 2}", xv)
                xc = xstrm[:, (fc % 2) * 1024:(fc % 2) * 1024 + 1024]
            else:
                if ready is not None:
                    st_, v_ = ready[fc]
                    P.wait("vector", st_, v_)
                xc = src_t[:, fc * TOK:(fc + 1) * TOK]
            for tg in range(2):
                xs = xc[:, tg * 512:(tg + 1) * 512]
                sl = (scr0 if tg == 0 else scr1)
                vop(P, "tensor_mul", sl[:, :], xs, a_t[:, tg * 512:(tg + 1) * 512])
                vop(P, "tensor_add",
                    xn_t[:, fc * TOK + tg * 512: fc * TOK + tg * 512 + 512],
                    sl[:, :], b_t[:, tg * 512:(tg + 1) * 512],
                    inc=("DVE", 1) if tg == 1 else None)
        return dve_ab

    def emit_ffn_preload(gb, ub, wc, pe_guard=None, dve_guard=None):
        eng_ld = "sync" if wc == "read" else "gpsimd"
        P.wait(eng_ld, "PE", P.n["PE"] if pe_guard is None else pe_guard)
        if dve_guard is not None:
            P.wait(eng_ld, "DVE", dve_guard)
        gu_v = {}
        cw_v = {}
        for ic in (0, 1):
            slot = ic % 2
            if wc == "read":
                dma(P, "sync", gb[:, slot * 2048:slot * 2048 + 2048],
                    c_gw[ic], f"RG{slot}")
                gu_v[ic] = dma(P, "sync", ub[:, slot * 2048:slot * 2048 + 2048],
                               c_uw[ic], f"RG{slot}")
            else:
                dma(P, "gpsimd", gb[:, slot * 2048:slot * 2048 + 2048], d_gw[ic],
                    f"G{slot}")
                gu_v[ic] = dma(P, "gpsimd", ub[:, slot * 2048:slot * 2048 + 2048],
                               d_uw[ic], f"G{slot}")
                if wc == "write":
                    P.wait("sync", f"G{slot}", gu_v[ic])
                    dma(P, "sync", c_gw[ic], gb[:, slot * 2048:slot * 2048 + 2048],
                        f"CW{slot}")
                    cw_v[ic] = dma(P, "sync", c_uw[ic],
                                   ub[:, slot * 2048:slot * 2048 + 2048], f"CW{slot}")
        return (gu_v, cw_v)

    def emit_ffn(xn_t, hdn_t, gb, ub, db, tgs, d_xsrc, mode, xn_base, wc=None,
                 pre=None):
        eng_ld = "sync" if wc == "read" else "gpsimd"
        P.wait(eng_ld, "PE", P.n["PE"])
        P.wait("scalar", "XC", xh_consts)
        if wc == "read":
            for s_ in ("CW0", "CW1", "CD0", "CD1"):
                P.wait("sync", s_, P.n[s_])
        ntg = len(tgs)
        W = ntg * 512
        act_ic = {}
        dve_ic = {}
        pe_idx = {}
        gu_v = {}
        cw_v = {}

        def load_gu(ic):
            slot = ic % 2
            if wc == "read":
                dma(P, "sync", gb[:, slot * 2048:slot * 2048 + 2048],
                    c_gw[ic], f"RG{slot}")
                gu_v[ic] = dma(P, "sync", ub[:, slot * 2048:slot * 2048 + 2048],
                               c_uw[ic], f"RG{slot}")
                return
            sem = f"G{slot}"
            dma(P, "gpsimd", gb[:, slot * 2048:slot * 2048 + 2048], d_gw[ic], sem)
            gu_v[ic] = dma(P, "gpsimd", ub[:, slot * 2048:slot * 2048 + 2048],
                           d_uw[ic], sem)
            if wc == "write":
                P.wait("sync", sem, gu_v[ic])
                dma(P, "sync", c_gw[ic], gb[:, slot * 2048:slot * 2048 + 2048],
                    f"CW{slot}")
                cw_v[ic] = dma(P, "sync", c_uw[ic],
                               ub[:, slot * 2048:slot * 2048 + 2048], f"CW{slot}")

        if pre is None:
            load_gu(0)
            load_gu(1)
        else:
            gu_v.update(pre[0])
            cw_v.update(pre[1])
        for ic in range(IC):
            bsel = (ic % 2) * 4
            slot = ic % 2
            P.wait("tensor", f"RG{slot}" if wc == "read" else f"G{slot}", gu_v[ic])
            if ic >= 2:
                P.wait("tensor", "ACT", act_ic[ic - 2])
                P.wait("tensor", "DVE", dve_ic[ic - 2])
            for kg in range(16):
                P.wait("tensor", "DVE", xn_base + kg + 1)
                start = (kg == 0)
                stop = (kg == 15)
                for ti in range(ntg):
                    tg = tgs[ti]
                    xs = xn_t[:, kg * TOK + tg * 512: kg * TOK + tg * 512 + 512]
                    mm(P, ps[bsel + ti][:, :],
                       gb[:, slot * 2048 + kg * 128: slot * 2048 + (kg + 1) * 128],
                       xs, start=start, stop=stop)
                    mm(P, ps[bsel + 2 + ti][:, :],
                       ub[:, slot * 2048 + kg * 128: slot * 2048 + (kg + 1) * 128],
                       xs, start=start, stop=stop,
                       inc=("PE", 1) if (kg == 15 and ti == ntg - 1) else None)
            pe_idx[ic] = P.n["PE"]
            if ic + 2 < IC:
                eng_pf = "sync" if wc == "read" else "gpsimd"
                P.wait(eng_pf, "PE", pe_idx[ic])
                if wc == "write":
                    P.wait("gpsimd", f"CW{slot}", cw_v[ic])
                load_gu(ic + 2)
            pe_ic = P.n["PE"]
            P.wait("scalar", "PE", pe_ic)
            if ic >= 1:
                P.wait("scalar", "DVE", dve_ic[ic - 1])
            for ti in range(ntg):
                act(P, (scr0 if ti == 0 else scr1)[:, :], ps[bsel + ti][:, :], AF.Silu,
                    scale=t_gs[:, ic:ic + 1], inc=("ACT", 1) if ti == ntg - 1 else None)
            act_ic[ic] = P.n["ACT"]
            P.wait("vector", "ACT", act_ic[ic])
            for ti in range(ntg):
                vop(P, "scalar_tensor_tensor",
                    hdn_t[:, ic * W + ti * 512: ic * W + (ti + 1) * 512],
                    ps[bsel + 2 + ti][:, :], t_us[:, ic:ic + 1],
                    (scr0 if ti == 0 else scr1)[:, :],
                    ALU.mult, ALU.mult, inc=("DVE", 1) if ti == ntg - 1 else None)
            dve_ic[ic] = P.n["DVE"]
        dve_hdn = P.n["DVE"]
        pe_d = {}
        dve_res = {}
        d_v = {}

        cd_v = {}

        def load_d(mc, hc2):
            idx = mc * 2 + hc2
            slot = idx % 2
            if wc == "read":
                d_v[idx] = dma(P, "sync", db[:, slot * 2816:slot * 2816 + 2816],
                               c_dw[mc, :, hc2 * 2816:hc2 * 2816 + 2816], f"RD{slot}")
                return
            d_v[idx] = dma(P, "gpsimd", db[:, slot * 2816:slot * 2816 + 2816],
                           d_dw[mc, :, hc2 * 2816:hc2 * 2816 + 2816], f"D{slot}")
            if wc == "write":
                P.wait("sync", f"D{slot}", d_v[idx])
                cd_v[idx] = dma(P, "sync", c_dw[mc, :, hc2 * 2816:hc2 * 2816 + 2816],
                                db[:, slot * 2816:slot * 2816 + 2816], f"CD{slot}")

        load_d(0, 0)
        load_d(0, 1)
        for mc in range(FC):
            bsel = (mc % 2) * 2
            for hc2 in range(2):
                idx = mc * 2 + hc2
                P.wait("tensor", f"RD{idx % 2}" if wc == "read" else f"D{idx % 2}",
                       d_v[idx])
                if idx == 0:
                    P.wait("tensor", "DVE", dve_hdn)
                if mc >= 2 and hc2 == 0:
                    P.wait("tensor", "DVE", dve_res[mc - 2])
                slot = idx % 2
                for kc in range(22):
                    kg = hc2 * 22 + kc
                    start = (hc2 == 0 and kc == 0)
                    stop = (hc2 == 1 and kc == 21)
                    for ti in range(ntg):
                        mm(P, ps[bsel + ti][:, :],
                           db[:, slot * 2816 + kc * 128: slot * 2816 + (kc + 1) * 128],
                           hdn_t[:, kg * W + ti * 512: kg * W + (ti + 1) * 512],
                           start=start, stop=stop,
                           inc=("PE", 1) if (kc == 21 and ti == ntg - 1) else None)
                pe_d[idx] = P.n["PE"]
                if idx + 2 < 2 * FC:
                    eng_pf = "sync" if wc == "read" else "gpsimd"
                    P.wait(eng_pf, "PE", pe_d[idx])
                    if wc == "write":
                        P.wait("gpsimd", f"CD{idx % 2}", cd_v[idx])
                    load_d((idx + 2) // 2, (idx + 2) % 2)
            pe_mc = P.n["PE"]
            if mode == "stream":
                if mc < 2:
                    P.wait("sync", "DVE", dve_hdn)
                else:
                    P.wait("sync", "DVE", dve_res[mc - 2])
                xv = dma(P, "sync", xstrm[:, (mc % 2) * 1024:(mc % 2) * 1024 + 1024],
                         d_xsrc[mc], f"X{mc % 2}")
                P.wait("vector", f"X{mc % 2}", xv)
            P.wait("vector", "PE", pe_mc)
            for ti in range(ntg):
                tg = tgs[ti]
                dst = resid[:, mc * TOK + tg * 512: mc * TOK + tg * 512 + 512]
                if mode == "stream":
                    other = xstrm[:, (mc % 2) * 1024 + tg * 512:
                                  (mc % 2) * 1024 + tg * 512 + 512]
                else:
                    other = dst
                vop(P, "scalar_tensor_tensor", dst, ps[bsel + ti][:, :],
                    t_hds[:, mc:mc + 1], other, ALU.mult, ALU.add,
                    inc=("DVE", 1) if ti == ntg - 1 else None)
            dve_res[mc] = P.n["DVE"]
        return dve_res, pe_idx[IC - 1]

    def emit_lat_preload(d_wsrc, MCH):
        P.wait("gpsimd", "PE", P.n["PE"])
        l_v = {}
        for m in range(min(2, MCH)):
            l_v[m] = dma(P, "gpsimd", latbuf[:, (m % 2) * 2048:(m % 2) * 2048 + 2048],
                         d_wsrc[m], f"L{m % 2}")
        return l_v

    def emit_lat(xn_t, xn_base, d_wsrc, MCH, dest_t, dest_off, pre=None):
        consumed = {}
        act_ev = {}
        if pre is None:
            P.wait("gpsimd", "PE", P.n["PE"])
            l_v = {}
        else:
            l_v = pre

        def load_l(m):
            slot = m % 2
            l_v[m] = dma(P, "gpsimd", latbuf[:, slot * 2048:slot * 2048 + 2048],
                         d_wsrc[m], f"L{slot}")

        if pre is None:
            load_l(0)
            if MCH > 1:
                load_l(1)
        for m in range(MCH):
            slot = m % 2
            P.wait("tensor", f"L{slot}", l_v[m])
            bsel = (m % 2) * 2
            if m >= 2:
                P.wait("tensor", "ACT", act_ev[m - 2])
            for kc in range(FC):
                if m == 0:
                    P.wait("tensor", "DVE", xn_base + kc + 1)
                for tg in range(2):
                    mm(P, ps[bsel + tg][:, :],
                       latbuf[:, slot * 2048 + kc * 128: slot * 2048 + (kc + 1) * 128],
                       xn_t[:, kc * TOK + tg * 512: kc * TOK + tg * 512 + 512],
                       start=(kc == 0), stop=(kc == FC - 1),
                       inc=("PE", 1) if (kc == FC - 1 and tg == 1) else None)
            consumed[m] = P.n["PE"]
            if m + 2 < MCH:
                P.wait("gpsimd", "PE", consumed[m])
                load_l(m + 2)
            P.wait("scalar", "PE", consumed[m])
            for tg in range(2):
                acopy(P, dest_t[:, dest_off(m, tg):dest_off(m, tg) + 512],
                      ps[bsel + tg][:, :], inc=("ACT", 1) if tg == 1 else None)
            act_ev[m] = P.n["ACT"]
        return P.n["ACT"]

    def finish(src_t):
        P.barrier(engs=("sync",))
        dma(P, "sync", d_y.rearrange("a p n -> p a n"),
            src_t[:, :].rearrange("p (a n) -> p a n", a=FC), "ST")
        P.wait("sync", "ST", P.n["ST"])

    preE = emit_ffn_preload(gbuf, ubuf, "write")
    b1 = emit_ln(d_xe, None, xn, abc, bbc, "hbm")
    resE, peE = emit_ffn(xn, hdn, gbuf, ubuf, dbuf, [0, 1], d_xe, "stream", b1,
                         wc="write", pre=preE)
    if upto == "ffnE":
        finish(resid)
        return _finalize(nc, P, root)
    preLE = emit_lat_preload(d_wkvd, KC)
    b2 = emit_ln(None, resid, xn2, abc2, bbc2, "sbuf",
                 ready={fc: ("DVE", resE[fc]) for fc in range(FC)})
    actkvE = emit_lat(xn2, b2, d_wkvd, KC, kvlat,
                      lambda m, tg: m * 2048 + 512 + tg * 512, pre=preLE)

    preO = emit_ffn_preload(gbuf, ubuf, "read", pe_guard=peE, dve_guard=b2 + FC)
    b3 = emit_ln(d_xo, None, xn, abc, bbc, "hbm")
    resO, peO = emit_ffn(xn, hdn, gbuf, ubuf, dbuf, [0, 1], d_xo, "stream", b3,
                         wc="read", pre=preO)
    if upto == "ffnO":
        finish(resid)
        return _finalize(nc, P, root)
    preLO = emit_lat_preload(d_wkvd, KC)
    b4 = emit_ln(None, resid, xn2, abc2, bbc2, "sbuf",
                 ready={fc: ("DVE", resO[fc]) for fc in range(FC)})
    actkvO = emit_lat(xn2, b4, d_wkvd, KC, kvlat,
                      lambda m, tg: m * 2048 + (0 if tg == 0 else 1536), pre=preLO)
    actql = emit_lat(xn2, b4, d_wqd, QC, qlat,
                     lambda m, tg: m * TOK + tg * 512)
    if dbg:
        P.wait("sync", "ACT", actql)
        dma(P, "sync", d_dkv.rearrange("a p n -> p a n"),
            kvlat[:, :].rearrange("p (a n) -> p a n", a=KC), "ST")
        dma(P, "sync", d_dql.rearrange("a p n -> p a n"),
            qlat[:, :].rearrange("p (a n) -> p a n", a=QC), "ST")
    if upto == "lat":
        finish(resid)
        return _finalize(nc, P, root)

    P.barrier(engs=("sync", "gpsimd"))
    kt_v = dma(P, "sync", ktab[:, :], d_ktab, "XC")
    qt_v = dma(P, "sync", qtab[:, :], d_qtab, "XC")

    order = [(g, h) for g in range(NG) for h in range(H)]
    aw_loaded = {}

    def load_aw(oi):
        buf = awb0 if oi % 2 == 0 else awb1
        aw_loaded[oi] = dma(P, "gpsimd", buf[:, :], d_aw[order[oi][1]], f"A{oi % 2}")

    load_aw(0)
    load_aw(1)

    tri_v = None
    pending_tail = None
    dve_evac = {"v": 0}
    wo_res_final = {}

    def emit_tail(pt):
        th, trec = pt
        P.wait("tensor", "DVE", trec)
        mm(P, ps[7][:, :], ones_f[0:1, 0:128], lrow[:, :], start=True, stop=True,
           inc=("PE", 1))
        P.wait("vector", "PE", P.n["PE"])
        vop(P, "reciprocal", invlb[:, :], ps[7][:, :], inc=("DVE", 1))
        P.wait("vector", "DVE", P.n["DVE"])
        r0 = th * 192
        t0, o0 = divmod(r0, 128)
        if o0 == 0:
            vop(P, "tensor_mul", aog[:, t0 * 512:(t0 + 1) * 512], ps[2][:, :],
                invlb[:, :])
            vop(P, "tensor_mul", aog[0:64, (t0 + 1) * 512:(t0 + 2) * 512],
                ps[3][0:64, :], invlb[0:64, :], inc=("DVE", 1))
        else:
            vop(P, "tensor_mul", aog[64:128, t0 * 512:(t0 + 1) * 512],
                ps[2][0:64, :], invlb[0:64, :])
            vop(P, "tensor_mul", aog[0:64, (t0 + 1) * 512:(t0 + 2) * 512],
                ps[2][64:128, :], invlb[64:128, :])
            vop(P, "tensor_mul", aog[64:128, (t0 + 1) * 512:(t0 + 2) * 512],
                ps[3][0:64, :], invlb[0:64, :], inc=("DVE", 1))
        dve_evac["v"] = P.n["DVE"]


    bank45_guard = [("ACT", 0), ("ACT", 0)]
    vproj_guard = {}
    pe_attn_hist = {-1: 0, -2: 0}
    dve_q_hist = {-1: 0}
    act_q_hist = {-1: 0}

    def vproj_chunks(oi2):
        g2, h2 = order[oi2]
        awb2 = awb0 if oi2 % 2 == 0 else awb1
        vbuf = vh if oi2 % 2 == 0 else vh2
        NTBP2 = EXT[g2] // 2
        guards = []
        AWV2 = 3072
        for tbp in range(NTBP2):
            bank = 4 + (tbp % 2)
            if tbp == 0:
                P.wait("tensor", f"A{oi2 % 2}", aw_loaded[oi2])
            if tbp < 2:
                P.wait("tensor", *bank45_guard[tbp % 2])
                if bank == 5:
                    P.wait("tensor", "ACT", act_q_hist[oi2 - 1])
            else:
                P.wait("tensor", *guards[tbp - 2])
            for sub in range(2):
                tb = tbp * 2 + sub
                for kc in range(KC):
                    mm(P, ps[bank][:, sub * 192:sub * 192 + 192],
                       kvlat[:, kc * 2048 + tb * 128: kc * 2048 + (tb + 1) * 128],
                       awb2[:, AWV2 + kc * 192: AWV2 + (kc + 1) * 192],
                       start=(kc == 0), stop=(kc == KC - 1),
                       inc=("PE", 1) if (kc == KC - 1 and sub == 1) else None)
            pvv = P.n["PE"]
            P.wait("scalar", "PE", pvv)
            P.wait("scalar", "PE", pe_attn_hist[oi2 - 2])
            acopy(P, vbuf[:, (tbp * 2) * VST:(tbp * 2) * VST + 192],
                  ps[bank][:, 0:192])
            acopy(P, vbuf[:, (tbp * 2 + 1) * VST:(tbp * 2 + 1) * VST + 192],
                  ps[bank][:, 192:384], inc=("ACT", 1))
            guards.append(("ACT", P.n["ACT"]))
            bank45_guard[tbp % 2] = guards[-1]
            vproj_guard[oi2] = list(guards)
            yield

    woven = {}

    for oi, (g, h) in enumerate(order):
        awb = awb0 if oi % 2 == 0 else awb1
        AWQ, AWK, AWV = 0, 2304, 3072
        if h == 0:
            P.wait("sync", "DVE", P.n["DVE"])
            tri_v = dma(P, "sync", tri_g[:, :], d_tri[g], "XC")
            if oi > 0:
                P.wait("gpsimd", "PE", P.n["PE"])
                load_aw(oi)
                load_aw(oi + 1)
            P.wait("vector", "PE", P.n["PE"])
            for tb in range(NKB):
                vop(P, "memset", vh[:, tb * VST + 192: tb * VST + 193], 1.0)
                vop(P, "memset", vh2[:, tb * VST + 192: tb * VST + 193], 1.0,
                    inc=("DVE", 1) if tb == NKB - 1 else None)
            P.wait("tensor", "DVE", P.n["DVE"])
            P.wait("vector", "XC", tri_v)
            P.wait("tensor", "ACT", max(actkvO, actkvE, actql))

        NGRP = EXT[g] // 4
        vbuf_self = vh if oi % 2 == 0 else vh2

        if not woven.get(oi):
            for _ in vproj_chunks(oi):
                pass

        act_head0 = P.n["ACT"]
        dve_head0 = P.n["DVE"]
        act_k_ev = {}
        dve_k = {}
        for ng in range(NGRP):
            bn = 5 if ng % 2 == 0 else 0
            br = 6 if ng % 2 == 0 else 1
            if ng % 2 == 0:
                if ng == 0:
                    for g_ in vproj_guard[oi]:
                        P.wait("tensor", *g_)
                    P.wait("tensor", *bank45_guard[1])
                    P.wait("tensor", "DVE", dve_q_hist[oi - 1])
                else:
                    P.wait("tensor", "ACT", act_k_ev[ng - 2])
                    P.wait("tensor", "DVE", dve_k[ng - 2])
            else:
                if ng == 1:
                    P.wait("tensor", "ACT", act_head0)
                    P.wait("tensor", "DVE", dve_head0)
                else:
                    P.wait("tensor", "ACT", act_k_ev[ng - 2])
                    P.wait("tensor", "DVE", dve_k[ng - 2])
            for kc in range(KC):
                mm(P, ps[bn][:, :],
                   awb[:, AWK + kc * 192: AWK + kc * 192 + 128],
                   kvlat[:, kc * 2048 + ng * 512: kc * 2048 + (ng + 1) * 512],
                   start=(kc == 0), stop=(kc == KC - 1))
                mm(P, ps[br][0:64, :],
                   awb[:, AWK + kc * 192 + 128: AWK + (kc + 1) * 192],
                   kvlat[:, kc * 2048 + ng * 512: kc * 2048 + (ng + 1) * 512],
                   start=(kc == 0), stop=(kc == KC - 1),
                   inc=("PE", 1) if kc == KC - 1 else None)
            pk = P.n["PE"]
            P.wait("scalar", "PE", pk)
            acopy(P, kh_a[:, ng * 512:(ng + 1) * 512], ps[bn][:, :], inc=("ACT", 1))
            act_k_ev[ng] = P.n["ACT"]
            P.wait("vector", "PE", pk)
            if ng >= 1:
                P.wait("vector", "DVE", dve_k[ng - 1])
            vop(P, "tensor_mul", ropeA[:, :], ps[br][0:64, :],
                ktab[0:64, ng * 512:(ng + 1) * 512])
            vop(P, "tensor_mul", ropeB[0:32, :], ps[br][32:64, :],
                ktab[64:96, ng * 512:(ng + 1) * 512])
            vop(P, "tensor_mul", ropeB[32:64, :], ps[br][0:32, :],
                ktab[64:96, ng * 512:(ng + 1) * 512], inc=("DVE", 1))
            P.wait("vector", "DVE", P.n["DVE"])
            vop(P, "tensor_sub", kh_b[0:32, ng * 512:(ng + 1) * 512],
                ropeA[0:32, :], ropeB[0:32, :])
            vop(P, "tensor_add", kh_b[32:64, ng * 512:(ng + 1) * 512],
                ropeA[32:64, :], ropeB[32:64, :], inc=("DVE", 1))
            dve_k[ng] = P.n["DVE"]

        P.wait("tensor", "ACT", act_k_ev[max(0, NGRP - 2)])
        P.wait("tensor", "DVE", dve_k[max(0, NGRP - 2)])
        for kc in range(QC):
            mm(P, ps[5][:, :], awb[:, AWQ + kc * 192: AWQ + kc * 192 + 128],
               qlat[:, kc * TOK + g * 512: kc * TOK + (g + 1) * 512],
               start=(kc == 0), stop=(kc == QC - 1))
            mm(P, ps[6][0:64, :], awb[:, AWQ + kc * 192 + 128: AWQ + (kc + 1) * 192],
               qlat[:, kc * TOK + g * 512: kc * TOK + (g + 1) * 512],
               start=(kc == 0), stop=(kc == QC - 1),
               inc=("PE", 1) if kc == QC - 1 else None)
        pq = P.n["PE"]
        P.wait("scalar", "PE", pq)
        acopy(P, qh_a[:, :], ps[5][:, :], inc=("ACT", 1))
        act_q_ev = P.n["ACT"]
        act_q_hist[oi] = act_q_ev
        P.wait("vector", "PE", pq)
        P.wait("vector", "DVE", dve_k[NGRP - 1])
        vop(P, "tensor_mul", ropeA[:, :], ps[6][0:64, :], qtab[0:64, g * 512:(g + 1) * 512])
        vop(P, "tensor_mul", ropeB[0:32, :], ps[6][32:64, :], qtab[64:96, g * 512:(g + 1) * 512])
        vop(P, "tensor_mul", ropeB[32:64, :], ps[6][0:32, :], qtab[64:96, g * 512:(g + 1) * 512],
            inc=("DVE", 1))
        P.wait("vector", "DVE", P.n["DVE"])
        vop(P, "tensor_sub", qh_b[0:32, :], ropeA[0:32, :], ropeB[0:32, :])
        vop(P, "tensor_add", qh_b[32:64, :], ropeA[32:64, :], ropeB[32:64, :],
            inc=("DVE", 1))
        dve_q = P.n["DVE"]
        dve_q_hist[oi] = dve_q

        if pending_tail is not None:
            emit_tail(pending_tail)
            pending_tail = None

        exp_done = {}
        pdone = {}

        def emit_scores(kb):
            sb = kb % 2
            if kb == 0:
                P.wait("tensor", "ACT", act_q_ev)
            if kb >= 2:
                P.wait("tensor", "ACT", exp_done[kb - 2])
            mm(P, ps[sb][:, :], kh_a[:, kb * 128:(kb + 1) * 128], qh_a[:, :],
               start=True, stop=False)
            if kb == 0:
                P.wait("tensor", "DVE", dve_q)
            mm(P, ps[sb][:, :], kh_b[:, kb * 128:(kb + 1) * 128], qh_b[:, :],
               start=False, stop=True, inc=("PE", 1))
            psc = P.n["PE"]
            P.wait("scalar", "PE", psc)
            pt = ptile[:, (kb % 3) * 512:(kb % 3) * 512 + 512]
            act(P, pt, ps[sb][:, :], AF.Exp,
                bias=t_badd[:, g * 16 + kb: g * 16 + kb + 1], scale=SCALE,
                inc=("ACT", 1))
            exp_done[kb] = P.n["ACT"]
            if DIAG[g] <= kb < DIAG[g] + 4:
                jj = kb - DIAG[g]
                P.wait("vector", "ACT", exp_done[kb])
                vop(P, "tensor_mul", pt, pt, tri_g[:, jj * 512:(jj + 1) * 512],
                    inc=("DVE", 1))
                pdone[kb] = ("DVE", P.n["DVE"])
            else:
                pdone[kb] = ("ACT", exp_done[kb])

        def emit_pv(kb):
            st, v = pdone[kb]
            P.wait("tensor", st, v)
            if kb == 0:
                P.wait("tensor", "DVE", dve_evac["v"])
                for g_ in vproj_guard[oi]:
                    P.wait("tensor", *g_)
            pt = ptile[:, (kb % 3) * 512:(kb % 3) * 512 + 512]
            mm(P, ps[2][:, :], vbuf_self[:, kb * VST: kb * VST + 128], pt,
               start=(kb == 0), stop=(kb == EXT[g] - 1))
            mm(P, ps[3][0:65, :], vbuf_self[:, kb * VST + 128: kb * VST + 193], pt,
               start=(kb == 0), stop=(kb == EXT[g] - 1),
               inc=("PE", 1) if kb == EXT[g] - 1 else None)

        weave = None
        if oi + 1 < len(order) and (oi + 1) % H != 0:
            weave = vproj_chunks(oi + 1)
            woven[oi + 1] = True

        def step_weave():
            if weave is not None:
                next(weave, None)

        for _ in range(4):
            step_weave()

        emit_scores(0)
        emit_scores(1)
        for kb in range(2, EXT[g]):
            emit_scores(kb)
            step_weave()
            emit_pv(kb - 2)
        step_weave()
        emit_pv(EXT[g] - 2)
        step_weave()
        emit_pv(EXT[g] - 1)
        if weave is not None:
            for _ in weave:
                pass
        pe_attn = P.n["PE"]
        pe_attn_hist[oi] = pe_attn

        if oi + 2 < len(order) and (oi + 2) % H not in (0, 1):
            P.wait("gpsimd", "PE", pe_attn)
            load_aw(oi + 2)

        P.wait("vector", "PE", pe_attn)
        vop(P, "tensor_copy", lrow[:, :], ps[3][64:65, :], inc=("DVE", 1))
        pending_tail = (h, P.n["DVE"])
        if h == H - 1:
            emit_tail(pending_tail)
            pending_tail = None

        if h == H - 1:
            dve_aog = P.n["DVE"]
            wo_res = {}
            P.wait("gpsimd", "PE", P.n["PE"])
            P.wait("gpsimd", "DVE", P.n["DVE"])
            wo_v = {}
            wo_pe = {}

            wo_bufs = (wob, wob1, wob2)
            wo_sems = ("WO", "W1", "W2")

            def load_wo(mc):
                wo_v[mc] = dma(P, "gpsimd", wo_bufs[mc % 3][:, :], d_wo[mc],
                               wo_sems[mc % 3])

            load_wo(0)
            load_wo(1)
            load_wo(2)
            for mc in range(FC):
                wbuf = wo_bufs[mc % 3]
                P.wait("tensor", wo_sems[mc % 3], wo_v[mc])
                if mc == 0:
                    P.wait("tensor", "DVE", dve_aog)
                bank = mc % 4
                if mc >= 4:
                    P.wait("tensor", "DVE", wo_res[mc - 4])
                for kc in range(24):
                    mm(P, ps[bank][:, :], wbuf[:, kc * 128:(kc + 1) * 128],
                       aog[:, kc * 512:(kc + 1) * 512],
                       start=(kc == 0), stop=(kc == 23),
                       inc=("PE", 1) if kc == 23 else None)
                pwo = P.n["PE"]
                wo_pe[mc] = pwo
                if mc + 3 < FC:
                    P.wait("gpsimd", "PE", pwo)
                    load_wo(mc + 3)
                P.wait("vector", "PE", pwo)
                dst = resid[:, mc * TOK + g * 512: mc * TOK + g * 512 + 512]
                vop(P, "tensor_add", dst, ps[bank][:, :], dst, inc=("DVE", 1))
                wo_res[mc] = P.n["DVE"]
                if g == NG - 1:
                    wo_res_final[mc] = P.n["DVE"]

    if upto == "attn":
        finish(resid)
        return _finalize(nc, P, root)

    preF = emit_ffn_preload(gbuf2, ubuf2, "read")
    b5 = emit_ln(None, resid, xn3, abc2, bbc2, "sbuf",
                 ready={fc: ("DVE", wo_res_final[fc]) for fc in range(FC)})
    emit_ffn(xn3, hdn2, gbuf2, ubuf2, dbuf2, [0], None, "inplace", b5, wc="read",
             pre=preF)
    emit_ffn(xn3, hdn2, gbuf2, ubuf2, dbuf2, [1], None, "inplace", b5, wc="read")
    del preF

    finish(resid)
    return _finalize(nc, P, root)


def _finalize(nc, P, root):
    with nc.Block() as block:
        @block.sync
        def _(e):
            for f in P.ops["sync"]:
                f(e)

        @block.gpsimd
        def _(e):
            for f in P.ops["gpsimd"]:
                f(e)

        @block.tensor
        def _(e):
            for f in P.ops["tensor"]:
                f(e)

        @block.vector
        def _(e):
            for f in P.ops["vector"]:
                f(e)

        @block.scalar
        def _(e):
            for f in P.ops["scalar"]:
                f(e)

    root.close()
    return nc



def _prep_shared(inp):
    f32 = lambda a: np.ascontiguousarray(np.asarray(a, dtype=np.float32))
    gate, up, down = f32(inp["L0_ffn_gate_w"]), f32(inp["L0_ffn_up_w"]), f32(inp["L0_ffn_down_w"])
    gw = gate.T.reshape(FC, 128, IC, 128).transpose(2, 1, 0, 3).reshape(IC, 128, FC * 128)
    uw = up.T.reshape(FC, 128, IC, 128).transpose(2, 1, 0, 3).reshape(IC, 128, FC * 128)
    dw = down.T.reshape(IC, 128, FC, 128).transpose(2, 1, 0, 3).reshape(FC, 128, IC * 128)
    gsc = f32(inp["L0_ffn_gate_s"]).reshape(IC, 128).T
    usc = f32(inp["L0_ffn_up_s"]).reshape(IC, 128).T
    dsc = f32(inp["L0_ffn_down_s"]).reshape(FC, 128).T
    wqd = f32(inp["L0_attn_wq_down"]).reshape(FC, 128, QC, 128).transpose(2, 1, 0, 3).reshape(QC, 128, FC * 128)
    wkvd = f32(inp["L0_attn_wkv_down"]).reshape(FC, 128, KC, 128).transpose(2, 1, 0, 3).reshape(KC, 128, FC * 128)
    wo = f32(inp["L0_attn_wo"]).reshape(24, 128, FC, 128).transpose(2, 1, 0, 3).reshape(FC, 128, 24 * 128)
    wqn, wqr = f32(inp["L0_attn_wq_nope"]), f32(inp["L0_attn_wq_rope"])
    wkn, wkr = f32(inp["L0_attn_wk_nope"]), f32(inp["L0_attn_wk_rope"])
    wvu = f32(inp["L0_attn_wv_up"])
    perm = np.concatenate([np.arange(0, 64, 2), np.arange(1, 64, 2)])
    aw = np.empty((H, 128, 3840), dtype=np.float32)
    for h in range(H):
        q = np.concatenate([wqn[:, h * 128:(h + 1) * 128],
                            wqr[:, h * 64:(h + 1) * 64][:, perm]], axis=1)
        k = np.concatenate([wkn[:, h * 128:(h + 1) * 128],
                            wkr[:, h * 64:(h + 1) * 64][:, perm]], axis=1)
        v = wvu[:, h * 192:(h + 1) * 192]
        aw[h, :, 0:2304] = q.reshape(QC, 128, 192).transpose(1, 0, 2).reshape(128, 2304)
        aw[h, :, 2304:3072] = k.reshape(KC, 128, 192).transpose(1, 0, 2).reshape(128, 768)
        aw[h, :, 3072:3840] = v.reshape(KC, 128, 192).transpose(1, 0, 2).reshape(128, 768)
    return dict(gw=gw, uw=uw, dw=dw, gsc=np.ascontiguousarray(gsc),
                usc=np.ascontiguousarray(usc), dsc=np.ascontiguousarray(dsc),
                wqd=wqd, wkvd=wkvd, wo=wo, aw=aw)


def _rope_tab(pos, width):
    inv = 1.0 / (10000.0 ** (np.arange(0, DR, 2, dtype=np.float64) / DR))
    ang = np.outer(pos.astype(np.float64), inv)
    cos = np.cos(ang).T.astype(np.float32)
    sin = np.sin(ang).T.astype(np.float32)
    tab = np.zeros((128, width), dtype=ml_dtypes.bfloat16)
    n = pos.shape[0]
    tab[0:32, :n] = cos
    tab[32:64, :n] = cos
    tab[64:96, :n] = sin
    return tab


def _core_layout(s):
    q = [np.arange(i * 512, (i + 1) * 512) for i in range(4)]
    if s == 0:
        pos_q = np.concatenate([q[0], q[3]])
        pos_e = np.concatenate([q[1], q[2]])
    else:
        pos_q = np.concatenate([q[1], q[2]])
        pos_e = np.concatenate([q[0], q[3]])
    pos_k = np.concatenate([pos_q[0:512], pos_e, pos_q[512:1024]])
    return pos_q, pos_e, pos_k


def _prep_core(inp, core):
    b, s = core // 2, core % 2
    x = np.asarray(inp["L0_attn_x"], dtype=np.float32)
    pos_q, pos_e, pos_k = _core_layout(s)
    own = np.ascontiguousarray(x[b][pos_q])
    extra = np.ascontiguousarray(x[b][pos_e])
    xo = np.ascontiguousarray(own.T.reshape(FC, 128, TOK))
    xe = np.ascontiguousarray(extra.T.reshape(FC, 128, TOK))
    ktab = _rope_tab(pos_k, 2048)
    qtab = _rope_tab(pos_q, TOK)
    tri = np.zeros((NG, 128, 4 * TGS), dtype=ml_dtypes.bfloat16)
    badd = np.zeros((128, 32), dtype=np.float32)
    for g in range(NG):
        qp = pos_q[g * TGS:(g + 1) * TGS]
        for kb in range(EXT[g]):
            kp = pos_k[kb * 128:(kb + 1) * 128]
            vis = kp[:, None] <= qp[None, :]
            if DIAG[g] <= kb < DIAG[g] + 4:
                jj = kb - DIAG[g]
                tri[g, :, jj * TGS:(jj + 1) * TGS] = vis.astype(ml_dtypes.bfloat16)
                badd[:, g * 16 + kb] = -MSHIFT
            elif not vis.any():
                badd[:, g * 16 + kb] = NEGBIG
            else:
                assert vis.all(), f"core{core} g{g} kb{kb}: partial mask outside diag"
                badd[:, g * 16 + kb] = -MSHIFT
    return dict(xo=xo, xe=xe, ktab=ktab, qtab=qtab, tri=tri, badd=badd)


def _assemble(results):
    out = np.empty((B, T, C), dtype=np.float32)
    for core in range(8):
        b, s = core // 2, core % 2
        pos_q, _, _ = _core_layout(s)
        y = np.asarray(results[core]["y"], dtype=np.float32)
        out[b, pos_q] = y.reshape(C, TOK).T
    return out


_NC_CACHE = {}


def kernel(**inputs):
    shared = _prep_shared(inputs)
    in_maps = []
    for core in range(8):
        m = dict(shared)
        m.update(_prep_core(inputs, core))
        in_maps.append(m)
    if "prog" not in _NC_CACHE:
        _NC_CACHE["prog"] = build_program()
    res = run_bass_kernel_spmd(_NC_CACHE["prog"], in_maps, core_ids=list(range(8)))
    return _assemble(res.results)


# version 23
# speedup vs baseline: 1.0777x; 1.0089x over previous
import numpy as np
import ml_dtypes
from contextlib import ExitStack

import concourse.bass as bass
import concourse.mybir as mybir
from concourse.bass_utils import run_bass_kernel_spmd

F32 = mybir.dt.float32
BF16 = mybir.dt.bfloat16
AF = mybir.ActivationFunctionType
ALU = mybir.AluOpType
AX = mybir.AxisListType

B, T, C = 4, 2048, 2048
FC = 16
INTER, IC = 5632, 44
QR, QC = 1536, 12
KVR, KC = 512, 4
H, DN, DR, D = 16, 128, 64, 192
TOK = 1024
TGS = 512
NG = 2
NKB = 16
SCALE = float(1.0 / np.sqrt(192.0))
EPS = 1e-5
NEGBIG = -1.0e30
MARGIN = 1.0625
MSHIFT = 40.0
EXT = {0: 8, 1: 16}
DIAG = {0: 0, 1: 12}

ENGINES = ("sync", "gpsimd", "tensor", "vector", "scalar")
STREAMS = ("XC", "X0", "X1", "X2", "ST", "G0", "G1", "D0", "D1", "L0", "L1",
           "A0", "A1", "WO", "W1", "W2", "CW0", "CW1", "CD0", "CD1", "RG0", "RG1", "RD0", "RD1", "PE", "ACT", "DVE")


class Prog:
    def __init__(self):
        self.ops = {e: [] for e in ENGINES}
        self.n = {s: 0 for s in STREAMS}
        self.hw = {e: {s: 0 for s in STREAMS} for e in ENGINES}
        self.sems = {}

    def emit(self, eng, fn):
        self.ops[eng].append(fn)

    def wait(self, eng, stream, val):
        if val <= 0 or val <= self.hw[eng][stream]:
            return
        self.hw[eng][stream] = val
        self.emit(eng, lambda e, s=stream, v=val: e.wait_ge(self.sems[s], v))

    def barrier(self, engs=ENGINES, streams=("PE", "ACT", "DVE")):
        for eng in engs:
            for s in streams:
                self.wait(eng, s, self.n[s])

    def bump(self, stream, k=1):
        self.n[stream] += k
        return self.n[stream]


def _bump_inc(P, inc):
    if inc is None:
        return None
    P.bump(inc[0], inc[1])
    return (inc[0], inc[1])


def _attach(ins, P, inc):
    if inc is not None:
        ins.then_inc(P.sems[inc[0]], inc[1])


def mm(P, out, lhsT, rhs, start, stop, inc=None):
    inc = _bump_inc(P, inc)

    def f(e, out=out, l=lhsT, r=rhs, st=start, sp=stop, inc=inc, P=P):
        _attach(e.matmul(out, l, r, start=st, stop=sp), P, inc)
    P.emit("tensor", f)


def act(P, out, in_, func, bias=0.0, scale=1.0, inc=None):
    inc = _bump_inc(P, inc)

    def f(e, out=out, i=in_, fu=func, b=bias, s=scale, inc=inc, P=P):
        _attach(e.activation(out, i, fu, bias=b, scale=s), P, inc)
    P.emit("scalar", f)


def acopy(P, out, in_, inc=None):
    inc = _bump_inc(P, inc)

    def f(e, out=out, i=in_, inc=inc, P=P):
        _attach(e.copy(out, i), P, inc)
    P.emit("scalar", f)


def vop(P, name, *args, inc=None, **kw):
    inc = _bump_inc(P, inc)

    def f(e, n=name, a=args, kw=kw, inc=inc, P=P):
        _attach(getattr(e, n)(*a, **kw), P, inc)
    P.emit("vector", f)


def dma(P, eng, out, in_, stream):
    v = P.bump(stream, 16)

    def f(e, out=out, i=in_, P=P, s=stream):
        e.dma_start(out=out, in_=i).then_inc(P.sems[s], 16)
    P.emit(eng, f)
    return v


def build_program(upto="all", dbg=False, race_detect=True):
    nc = bass.Bass("TRN2", target_bir_lowering=False,
                   detect_race_conditions=race_detect)
    P = Prog()

    d_xo = nc.dram_tensor("xo", [FC, 128, TOK], F32, kind="ExternalInput").ap()
    d_xe = nc.dram_tensor("xe", [FC, 128, TOK], F32, kind="ExternalInput").ap()
    d_gw = nc.dram_tensor("gw", [IC, 128, FC * 128], F32, kind="ExternalInput").ap()
    d_uw = nc.dram_tensor("uw", [IC, 128, FC * 128], F32, kind="ExternalInput").ap()
    d_dw = nc.dram_tensor("dw", [FC, 128, IC * 128], F32, kind="ExternalInput").ap()
    d_gs = nc.dram_tensor("gsc", [128, IC], F32, kind="ExternalInput").ap()
    d_us = nc.dram_tensor("usc", [128, IC], F32, kind="ExternalInput").ap()
    d_ds = nc.dram_tensor("dsc", [128, FC], F32, kind="ExternalInput").ap()
    d_wqd = nc.dram_tensor("wqd", [QC, 128, FC * 128], F32, kind="ExternalInput").ap()
    d_wkvd = nc.dram_tensor("wkvd", [KC, 128, FC * 128], F32, kind="ExternalInput").ap()
    d_aw = nc.dram_tensor("aw", [H, 128, 3840], F32, kind="ExternalInput").ap()
    d_wo = nc.dram_tensor("wo", [FC, 128, 3072], F32, kind="ExternalInput").ap()
    d_ktab = nc.dram_tensor("ktab", [128, 2048], BF16, kind="ExternalInput").ap()
    d_qtab = nc.dram_tensor("qtab", [128, TOK], BF16, kind="ExternalInput").ap()
    d_tri = nc.dram_tensor("tri", [NG, 128, 4 * TGS], BF16, kind="ExternalInput").ap()
    d_badd = nc.dram_tensor("badd", [128, 32], F32, kind="ExternalInput").ap()
    d_y = nc.dram_tensor("y", [FC, 128, TOK], F32, kind="ExternalOutput").ap()
    c_gw = nc.dram_tensor("c_gw", [IC, 128, FC * 128], BF16, kind="Internal").ap()
    c_uw = nc.dram_tensor("c_uw", [IC, 128, FC * 128], BF16, kind="Internal").ap()
    c_dw = nc.dram_tensor("c_dw", [FC, 128, IC * 128], BF16, kind="Internal").ap()
    if dbg:
        d_dkv = nc.dram_tensor("dbg_kv", [KC, 128, 2048], BF16, kind="ExternalOutput").ap()
        d_dql = nc.dram_tensor("dbg_ql", [QC, 128, TOK], BF16, kind="ExternalOutput").ap()
        d_dat = nc.dram_tensor("dbg_at", [128, 8192], F32, kind="ExternalOutput").ap()
        d_dao = nc.dram_tensor("dbg_ao", [128, 12288], BF16, kind="ExternalOutput").ap()

    B0 = 16512
    assert B0 >= nc.sbuf_base
    TOPB = nc.sbuf_top

    def S(name, shape, dtype, off):
        assert off % 32 == 0, name
        nbytes = shape[1] * (4 if dtype == F32 else 2)
        assert B0 <= off and off + nbytes <= TOPB, (name, off, nbytes, off + nbytes)
        return nc.alloc_sbuf_tensor_at(name, shape, dtype, offset=off)

    ones_bf = S("ones_bf", [128, 128], BF16, B0 + 0)
    ones_f = S("ones_f", [128, 128], F32, B0 + 256)
    t_gs = S("t_gs", [128, IC], F32, B0 + 768)
    t_us = S("t_us", [128, IC], F32, B0 + 960)
    t_ds = S("t_ds", [128, FC], F32, B0 + 1152)
    t_hds = S("t_hds", [128, FC], F32, B0 + 1216)
    t_badd = S("t_badd", [128, 32], F32, B0 + 1280)
    scr0 = S("scr0", [128, 512], F32, B0 + 1408)
    scr1 = S("scr1", [128, 512], F32, B0 + 3456)
    scrb0 = S("scrb0", [128, 1024], BF16, B0 + 1408)
    scrb1 = S("scrb1", [128, 1024], BF16, B0 + 3456)
    scrb0b = S("scrb0b", [128, 1024], BF16, B0 + 13696)
    scrb1b = S("scrb1b", [128, 1024], BF16, B0 + 15744)
    mrow = S("mrow", [1, 512], F32, B0 + 5504)
    vrow = S("vrow", [1, 512], F32, B0 + 7552)
    brow = S("brow", [1, 512], F32, B0 + 9600)
    trow = S("trow", [1, 512], F32, B0 + 11648)
    lrow = S("lrow", [1, 512], F32, B0 + 13696)
    irow = S("irow", [1, 512], F32, B0 + 15744)
    kvlat = S("kvlat", [128, KC * 2048], BF16, B0 + 17792)

    A0 = B0 + 34176

    xstrm = S("xstrm", [128, 2048], F32, A0 + 0)
    resid = S("resid", [128, FC * TOK], F32, A0 + 8192)
    xn = S("xn", [128, FC * TOK], BF16, A0 + 8192)
    gbuf = S("gbuf", [128, 4096], BF16, A0 + 40960)
    ubuf = S("ubuf", [128, 4096], BF16, A0 + 49152)
    abc = S("abc", [128, TOK], F32, A0 + 57344)
    bbc = S("bbc", [128, TOK], F32, A0 + 61440)
    hdn = S("hdn", [128, IC * TOK], BF16, A0 + 73728)
    dbuf = S("dbuf", [128, IC * 128], BF16, A0 + 163840)
    abc2 = S("abc2", [128, TOK], F32, A0 + 0)
    bbc2 = S("bbc2", [128, TOK], F32, A0 + 4096)

    xn2 = S("xn2", [128, FC * TOK], BF16, A0 + 73728)
    latbuf = S("latbuf", [128, 4096], BF16, A0 + 106496)
    qlat = S("qlat", [128, QC * TOK], BF16, A0 + 114688)

    ktab = S("ktab", [128, 2048], BF16, A0 + 73728)
    qtab = S("qtab", [128, TOK], BF16, A0 + 77824)
    tri_g = S("tri_g", [128, 2048], BF16, A0 + 79872)
    kh_a = S("kh_a", [128, 2048], BF16, A0 + 83968)
    kh_b = S("kh_b", [64, 2048], BF16, A0 + 88064)
    qh_a = S("qh_a", [128, 512], BF16, A0 + 92160)
    qh_b = S("qh_b", [64, 512], BF16, A0 + 93184)
    VST = 194
    vh = S("vh", [128, NKB * VST], BF16, A0 + 94208)
    vh2 = S("vh2", [128, NKB * VST], BF16, A0 + 100416)
    ropeA = S("ropeA", [64, 512], F32, A0 + 106624)
    ropeB = S("ropeB", [64, 512], F32, A0 + 108672)
    invlb = S("invlb", [128, 512], F32, A0 + 110720)
    wob = S("wob", [128, 3072], BF16, A0 + 83968)
    wob1 = S("wob1", [128, 3072], BF16, A0 + 163840)
    wob2 = S("wob2", [128, 3072], BF16, A0 + 94208)
    aog = S("aog", [128, 24 * TGS], BF16, A0 + 139264)
    awb0 = S("awb0", [128, 3840], BF16, A0 + 163840)
    ptile = S("ptile", [128, 3 * 512], BF16, A0 + 171520)
    awb1 = S("awb1", [128, 3840], BF16, A0 + 0)

    xn3 = S("xn3", [128, FC * TOK], BF16, A0 + 73728)
    gbuf2 = S("gbuf2", [128, 4096], BF16, A0 + 106496)
    ubuf2 = S("ubuf2", [128, 4096], BF16, B0 + 17792)
    hdn2 = S("hdn2", [128, IC * TGS], BF16, A0 + 114688)
    dbuf2 = S("dbuf2", [128, IC * 128], BF16, A0 + 159744)

    ps = [nc.alloc_psum_tensor(f"ps{i}", [128, 512], F32) for i in range(8)]
    nc._dbg_tensors = dict(resid=resid, xn=xn, abc=abc, bbc=bbc, hdn=hdn, xn2=xn2,
                           kvlat=kvlat, qlat=qlat, scr0=scr0, scr1=scr1, mrow=mrow,
                           vrow=vrow, brow=brow, xstrm=xstrm, kh_a=kh_a, kh_b=kh_b,
                           qh_a=qh_a, qh_b=qh_b, vh=vh, vh2=vh2, aog=aog,
                           invlb=invlb, xn3=xn3, hdn2=hdn2, ptile=ptile)

    root = ExitStack()
    for s in STREAMS:
        P.sems[s] = root.enter_context(nc.semaphore(s))
    root.enter_context(nc.allow_low_precision("bf16 matmuls allowed by problem tolerance"))

    for dst, dsrc in ((t_gs, d_gs), (t_us, d_us), (t_ds, d_ds), (t_badd, d_badd)):
        dma(P, "sync", dst[:, :], dsrc, "XC")
    xh_consts = P.n["XC"]
    vop(P, "memset", ones_bf[:, :], 1.0)
    vop(P, "memset", ones_f[:, :], 1.0, inc=("DVE", 1))
    P.wait("vector", "XC", xh_consts)
    vop(P, "tensor_scalar_mul", t_hds[:, :], t_ds[:, :], 0.5, inc=("DVE", 1))

    def emit_ln(d_src, src_t, xn_t, a_t, b_t, src_kind, ready=None):
        P.wait("scalar", "XC", xh_consts)
        pe_b = P.n["PE"]
        act_b = P.n["ACT"]
        dve_b = P.n["DVE"]
        dve_entry = P.n["DVE"]
        for fc in range(FC):
            if src_kind == "hbm":
                ldq = "sync" if fc % 2 == 0 else "scalar"
                sm = "X0" if fc % 2 == 0 else "X2"
                if fc >= 2:
                    P.wait(ldq, "PE", pe_b + fc - 1)
                else:
                    P.wait(ldq, "DVE", dve_entry)
                xv = dma(P, ldq, xstrm[:, (fc % 2) * 1024:(fc % 2) * 1024 + 1024],
                         d_src[fc], sm)
                P.wait("tensor", sm, xv)
                P.wait("scalar", sm, xv)
                P.wait("vector", sm, xv)
                xc = xstrm[:, (fc % 2) * 1024:(fc % 2) * 1024 + 1024]
            else:
                if ready is not None:
                    st_, v_ = ready[fc]
                    P.wait("scalar", st_, v_)
                    P.wait("vector", st_, v_)
                    P.wait("tensor", st_, v_)
                xc = src_t[:, fc * TOK:(fc + 1) * TOK]
            sq = scrb0 if fc % 2 == 0 else scrb0b
            sx = scrb1 if fc % 2 == 0 else scrb1b
            if fc >= 2:
                P.wait("scalar", "PE", pe_b + fc - 1)
                P.wait("vector", "PE", pe_b + fc - 1)
            act(P, sq[:, 0:512], xc[:, 0:512], AF.Square)
            act(P, sq[:, 512:1024], xc[:, 512:1024], AF.Square, inc=("ACT", 1))
            vop(P, "tensor_copy", sx[:, :], xc[:, :], inc=("DVE", 1))
            P.wait("tensor", "ACT", act_b + fc + 1)
            P.wait("tensor", "DVE", dve_b + fc + 1)
            for tg in range(2):
                mm(P, ps[4 + tg][0:1, :], ones_bf[:, 0:1],
                   sx[:, tg * 512:(tg + 1) * 512],
                   start=(fc == 0), stop=(fc == FC - 1))
                mm(P, ps[6 + tg][0:1, :], ones_bf[:, 0:1],
                   sq[:, tg * 512:(tg + 1) * 512],
                   start=(fc == 0), stop=(fc == FC - 1),
                   inc=("PE", 1) if tg == 1 else None)
        pe_stats = P.n["PE"]
        for tg in range(2):
            P.wait("vector", "PE", pe_stats)
            vop(P, "tensor_scalar_mul", mrow[:, :], ps[4 + tg][0:1, :], 1.0 / C)
            vop(P, "tensor_scalar", vrow[:, :], ps[6 + tg][0:1, :], 1.0 / C, EPS,
                ALU.mult, ALU.add)
            vop(P, "tensor_mul", trow[:, :], mrow[:, :], mrow[:, :])
            vop(P, "tensor_sub", vrow[:, :], vrow[:, :], trow[:, :], inc=("DVE", 1))
            P.wait("scalar", "DVE", P.n["DVE"])
            act(P, trow[:, :], vrow[:, :], AF.Ln)
            act(P, vrow[:, :], trow[:, :], AF.Exp, scale=-0.5, inc=("ACT", 1))
            P.wait("vector", "ACT", P.n["ACT"])
            vop(P, "scalar_tensor_tensor", brow[:, :], mrow[:, :], -1.0, vrow[:, :],
                ALU.mult, ALU.mult, inc=("DVE", 1))
            P.wait("tensor", "DVE", P.n["DVE"])
            mm(P, ps[0 + tg][:, :], ones_f[0:1, 0:128], vrow[:, :], start=True, stop=True)
            mm(P, ps[2 + tg][:, :], ones_f[0:1, 0:128], brow[:, :], start=True, stop=True,
               inc=("PE", 1))
            P.wait("vector", "PE", P.n["PE"])
            vop(P, "tensor_copy", a_t[:, tg * 512:(tg + 1) * 512], ps[0 + tg][:, :])
            vop(P, "tensor_copy", b_t[:, tg * 512:(tg + 1) * 512], ps[2 + tg][:, :],
                inc=("DVE", 1))
        dve_ab = P.n["DVE"]
        for fc in range(FC):
            if src_kind == "hbm":
                ldq = "sync" if fc % 2 == 0 else "scalar"
                sm = "X0" if fc % 2 == 0 else "X2"
                if fc < 2:
                    P.wait(ldq, "PE", pe_stats)
                else:
                    P.wait(ldq, "DVE", dve_ab + fc - 1)
                xv = dma(P, ldq, xstrm[:, (fc % 2) * 1024:(fc % 2) * 1024 + 1024],
                         d_src[fc], sm)
                P.wait("vector", sm, xv)
                xc = xstrm[:, (fc % 2) * 1024:(fc % 2) * 1024 + 1024]
            else:
                if ready is not None:
                    st_, v_ = ready[fc]
                    P.wait("vector", st_, v_)
                xc = src_t[:, fc * TOK:(fc + 1) * TOK]
            for tg in range(2):
                xs = xc[:, tg * 512:(tg + 1) * 512]
                sl = (scr0 if tg == 0 else scr1)
                vop(P, "tensor_mul", sl[:, :], xs, a_t[:, tg * 512:(tg + 1) * 512])
                vop(P, "tensor_add",
                    xn_t[:, fc * TOK + tg * 512: fc * TOK + tg * 512 + 512],
                    sl[:, :], b_t[:, tg * 512:(tg + 1) * 512],
                    inc=("DVE", 1) if tg == 1 else None)
        return dve_ab

    def emit_ffn_preload(gb, ub, wc, pe_guard=None, dve_guard=None):
        eng_ld = "sync" if wc == "read" else "gpsimd"
        P.wait(eng_ld, "PE", P.n["PE"] if pe_guard is None else pe_guard)
        if dve_guard is not None:
            P.wait(eng_ld, "DVE", dve_guard)
        gu_v = {}
        cw_v = {}
        for ic in (0, 1):
            slot = ic % 2
            if wc == "read":
                dma(P, "sync", gb[:, slot * 2048:slot * 2048 + 2048],
                    c_gw[ic], f"RG{slot}")
                gu_v[ic] = dma(P, "sync", ub[:, slot * 2048:slot * 2048 + 2048],
                               c_uw[ic], f"RG{slot}")
            else:
                dma(P, "gpsimd", gb[:, slot * 2048:slot * 2048 + 2048], d_gw[ic],
                    f"G{slot}")
                gu_v[ic] = dma(P, "gpsimd", ub[:, slot * 2048:slot * 2048 + 2048],
                               d_uw[ic], f"G{slot}")
                if wc == "write":
                    P.wait("sync", f"G{slot}", gu_v[ic])
                    dma(P, "sync", c_gw[ic], gb[:, slot * 2048:slot * 2048 + 2048],
                        f"CW{slot}")
                    cw_v[ic] = dma(P, "sync", c_uw[ic],
                                   ub[:, slot * 2048:slot * 2048 + 2048], f"CW{slot}")
        return (gu_v, cw_v)

    def emit_ffn(xn_t, hdn_t, gb, ub, db, tgs, d_xsrc, mode, xn_base, wc=None,
                 pre=None):
        eng_ld = "sync" if wc == "read" else "gpsimd"
        P.wait(eng_ld, "PE", P.n["PE"])
        P.wait("scalar", "XC", xh_consts)
        if wc == "read":
            for s_ in ("CW0", "CW1", "CD0", "CD1"):
                P.wait("sync", s_, P.n[s_])
        ntg = len(tgs)
        W = ntg * 512
        act_ic = {}
        dve_ic = {}
        pe_idx = {}
        gu_v = {}
        cw_v = {}

        def load_gu(ic):
            slot = ic % 2
            if wc == "read":
                dma(P, "sync", gb[:, slot * 2048:slot * 2048 + 2048],
                    c_gw[ic], f"RG{slot}")
                gu_v[ic] = dma(P, "sync", ub[:, slot * 2048:slot * 2048 + 2048],
                               c_uw[ic], f"RG{slot}")
                return
            sem = f"G{slot}"
            dma(P, "gpsimd", gb[:, slot * 2048:slot * 2048 + 2048], d_gw[ic], sem)
            gu_v[ic] = dma(P, "gpsimd", ub[:, slot * 2048:slot * 2048 + 2048],
                           d_uw[ic], sem)
            if wc == "write":
                P.wait("sync", sem, gu_v[ic])
                dma(P, "sync", c_gw[ic], gb[:, slot * 2048:slot * 2048 + 2048],
                    f"CW{slot}")
                cw_v[ic] = dma(P, "sync", c_uw[ic],
                               ub[:, slot * 2048:slot * 2048 + 2048], f"CW{slot}")

        if pre is None:
            load_gu(0)
            load_gu(1)
        else:
            gu_v.update(pre[0])
            cw_v.update(pre[1])
        for ic in range(IC):
            bsel = (ic % 2) * 4
            slot = ic % 2
            P.wait("tensor", f"RG{slot}" if wc == "read" else f"G{slot}", gu_v[ic])
            if ic >= 2:
                P.wait("tensor", "ACT", act_ic[ic - 2])
                P.wait("tensor", "DVE", dve_ic[ic - 2])
            for kg in range(16):
                P.wait("tensor", "DVE", xn_base + kg + 1)
                start = (kg == 0)
                stop = (kg == 15)
                for ti in range(ntg):
                    tg = tgs[ti]
                    xs = xn_t[:, kg * TOK + tg * 512: kg * TOK + tg * 512 + 512]
                    mm(P, ps[bsel + ti][:, :],
                       gb[:, slot * 2048 + kg * 128: slot * 2048 + (kg + 1) * 128],
                       xs, start=start, stop=stop)
                    mm(P, ps[bsel + 2 + ti][:, :],
                       ub[:, slot * 2048 + kg * 128: slot * 2048 + (kg + 1) * 128],
                       xs, start=start, stop=stop,
                       inc=("PE", 1) if (kg == 15 and ti == ntg - 1) else None)
            pe_idx[ic] = P.n["PE"]
            if ic + 2 < IC:
                eng_pf = "sync" if wc == "read" else "gpsimd"
                P.wait(eng_pf, "PE", pe_idx[ic])
                if wc == "write":
                    P.wait("gpsimd", f"CW{slot}", cw_v[ic])
                load_gu(ic + 2)
            pe_ic = P.n["PE"]
            P.wait("scalar", "PE", pe_ic)
            if ic >= 1:
                P.wait("scalar", "DVE", dve_ic[ic - 1])
            for ti in range(ntg):
                act(P, (scr0 if ti == 0 else scr1)[:, :], ps[bsel + ti][:, :], AF.Silu,
                    scale=t_gs[:, ic:ic + 1], inc=("ACT", 1) if ti == ntg - 1 else None)
            act_ic[ic] = P.n["ACT"]
            P.wait("vector", "ACT", act_ic[ic])
            for ti in range(ntg):
                vop(P, "scalar_tensor_tensor",
                    hdn_t[:, ic * W + ti * 512: ic * W + (ti + 1) * 512],
                    ps[bsel + 2 + ti][:, :], t_us[:, ic:ic + 1],
                    (scr0 if ti == 0 else scr1)[:, :],
                    ALU.mult, ALU.mult, inc=("DVE", 1) if ti == ntg - 1 else None)
            dve_ic[ic] = P.n["DVE"]
        dve_hdn = P.n["DVE"]
        pe_d = {}
        dve_res = {}
        d_v = {}

        cd_v = {}

        def load_d(mc, hc2):
            idx = mc * 2 + hc2
            slot = idx % 2
            if wc == "read":
                d_v[idx] = dma(P, "sync", db[:, slot * 2816:slot * 2816 + 2816],
                               c_dw[mc, :, hc2 * 2816:hc2 * 2816 + 2816], f"RD{slot}")
                return
            d_v[idx] = dma(P, "gpsimd", db[:, slot * 2816:slot * 2816 + 2816],
                           d_dw[mc, :, hc2 * 2816:hc2 * 2816 + 2816], f"D{slot}")
            if wc == "write":
                P.wait("sync", f"D{slot}", d_v[idx])
                cd_v[idx] = dma(P, "sync", c_dw[mc, :, hc2 * 2816:hc2 * 2816 + 2816],
                                db[:, slot * 2816:slot * 2816 + 2816], f"CD{slot}")

        load_d(0, 0)
        load_d(0, 1)
        for mc in range(FC):
            bsel = (mc % 2) * 2
            for hc2 in range(2):
                idx = mc * 2 + hc2
                P.wait("tensor", f"RD{idx % 2}" if wc == "read" else f"D{idx % 2}",
                       d_v[idx])
                if idx == 0:
                    P.wait("tensor", "DVE", dve_hdn)
                if mc >= 2 and hc2 == 0:
                    P.wait("tensor", "DVE", dve_res[mc - 2])
                slot = idx % 2
                for kc in range(22):
                    kg = hc2 * 22 + kc
                    start = (hc2 == 0 and kc == 0)
                    stop = (hc2 == 1 and kc == 21)
                    for ti in range(ntg):
                        mm(P, ps[bsel + ti][:, :],
                           db[:, slot * 2816 + kc * 128: slot * 2816 + (kc + 1) * 128],
                           hdn_t[:, kg * W + ti * 512: kg * W + (ti + 1) * 512],
                           start=start, stop=stop,
                           inc=("PE", 1) if (kc == 21 and ti == ntg - 1) else None)
                pe_d[idx] = P.n["PE"]
                if idx + 2 < 2 * FC:
                    eng_pf = "sync" if wc == "read" else "gpsimd"
                    P.wait(eng_pf, "PE", pe_d[idx])
                    if wc == "write":
                        P.wait("gpsimd", f"CD{idx % 2}", cd_v[idx])
                    load_d((idx + 2) // 2, (idx + 2) % 2)
            pe_mc = P.n["PE"]
            if mode == "stream":
                if mc < 2:
                    P.wait("sync", "DVE", dve_hdn)
                else:
                    P.wait("sync", "DVE", dve_res[mc - 2])
                xv = dma(P, "sync", xstrm[:, (mc % 2) * 1024:(mc % 2) * 1024 + 1024],
                         d_xsrc[mc], f"X{mc % 2}")
                P.wait("vector", f"X{mc % 2}", xv)
            P.wait("vector", "PE", pe_mc)
            for ti in range(ntg):
                tg = tgs[ti]
                dst = resid[:, mc * TOK + tg * 512: mc * TOK + tg * 512 + 512]
                if mode == "stream":
                    other = xstrm[:, (mc % 2) * 1024 + tg * 512:
                                  (mc % 2) * 1024 + tg * 512 + 512]
                else:
                    other = dst
                vop(P, "scalar_tensor_tensor", dst, ps[bsel + ti][:, :],
                    t_hds[:, mc:mc + 1], other, ALU.mult, ALU.add,
                    inc=("DVE", 1) if ti == ntg - 1 else None)
            dve_res[mc] = P.n["DVE"]
        return dve_res, pe_idx[IC - 1]

    def emit_lat_preload(d_wsrc, MCH):
        P.wait("gpsimd", "PE", P.n["PE"])
        l_v = {}
        for m in range(min(2, MCH)):
            l_v[m] = dma(P, "gpsimd", latbuf[:, (m % 2) * 2048:(m % 2) * 2048 + 2048],
                         d_wsrc[m], f"L{m % 2}")
        return l_v

    def emit_lat(xn_t, xn_base, d_wsrc, MCH, dest_t, dest_off, pre=None):
        consumed = {}
        act_ev = {}
        if pre is None:
            P.wait("gpsimd", "PE", P.n["PE"])
            l_v = {}
        else:
            l_v = pre

        def load_l(m):
            slot = m % 2
            l_v[m] = dma(P, "gpsimd", latbuf[:, slot * 2048:slot * 2048 + 2048],
                         d_wsrc[m], f"L{slot}")

        if pre is None:
            load_l(0)
            if MCH > 1:
                load_l(1)
        for m in range(MCH):
            slot = m % 2
            P.wait("tensor", f"L{slot}", l_v[m])
            bsel = (m % 2) * 2
            if m >= 2:
                P.wait("tensor", "ACT", act_ev[m - 2])
            for kc in range(FC):
                if m == 0:
                    P.wait("tensor", "DVE", xn_base + kc + 1)
                for tg in range(2):
                    mm(P, ps[bsel + tg][:, :],
                       latbuf[:, slot * 2048 + kc * 128: slot * 2048 + (kc + 1) * 128],
                       xn_t[:, kc * TOK + tg * 512: kc * TOK + tg * 512 + 512],
                       start=(kc == 0), stop=(kc == FC - 1),
                       inc=("PE", 1) if (kc == FC - 1 and tg == 1) else None)
            consumed[m] = P.n["PE"]
            if m + 2 < MCH:
                P.wait("gpsimd", "PE", consumed[m])
                load_l(m + 2)
            P.wait("scalar", "PE", consumed[m])
            for tg in range(2):
                acopy(P, dest_t[:, dest_off(m, tg):dest_off(m, tg) + 512],
                      ps[bsel + tg][:, :], inc=("ACT", 1) if tg == 1 else None)
            act_ev[m] = P.n["ACT"]
        return P.n["ACT"]

    def finish(src_t):
        P.barrier(engs=("sync",))
        dma(P, "sync", d_y.rearrange("a p n -> p a n"),
            src_t[:, :].rearrange("p (a n) -> p a n", a=FC), "ST")
        P.wait("sync", "ST", P.n["ST"])

    preE = emit_ffn_preload(gbuf, ubuf, "write")
    b1 = emit_ln(d_xe, None, xn, abc, bbc, "hbm")
    resE, peE = emit_ffn(xn, hdn, gbuf, ubuf, dbuf, [0, 1], d_xe, "stream", b1,
                         wc="write", pre=preE)
    if upto == "ffnE":
        finish(resid)
        return _finalize(nc, P, root)
    preLE = emit_lat_preload(d_wkvd, KC)
    b2 = emit_ln(None, resid, xn2, abc2, bbc2, "sbuf",
                 ready={fc: ("DVE", resE[fc]) for fc in range(FC)})
    actkvE = emit_lat(xn2, b2, d_wkvd, KC, kvlat,
                      lambda m, tg: m * 2048 + 512 + tg * 512, pre=preLE)

    preO = emit_ffn_preload(gbuf, ubuf, "read", pe_guard=peE, dve_guard=b2 + FC)
    b3 = emit_ln(d_xo, None, xn, abc, bbc, "hbm")
    resO, peO = emit_ffn(xn, hdn, gbuf, ubuf, dbuf, [0, 1], d_xo, "stream", b3,
                         wc="read", pre=preO)
    if upto == "ffnO":
        finish(resid)
        return _finalize(nc, P, root)
    preLO = emit_lat_preload(d_wkvd, KC)
    b4 = emit_ln(None, resid, xn2, abc2, bbc2, "sbuf",
                 ready={fc: ("DVE", resO[fc]) for fc in range(FC)})
    actkvO = emit_lat(xn2, b4, d_wkvd, KC, kvlat,
                      lambda m, tg: m * 2048 + (0 if tg == 0 else 1536), pre=preLO)
    actql = emit_lat(xn2, b4, d_wqd, QC, qlat,
                     lambda m, tg: m * TOK + tg * 512)
    if dbg:
        P.wait("sync", "ACT", actql)
        dma(P, "sync", d_dkv.rearrange("a p n -> p a n"),
            kvlat[:, :].rearrange("p (a n) -> p a n", a=KC), "ST")
        dma(P, "sync", d_dql.rearrange("a p n -> p a n"),
            qlat[:, :].rearrange("p (a n) -> p a n", a=QC), "ST")
    if upto == "lat":
        finish(resid)
        return _finalize(nc, P, root)

    P.barrier(engs=("sync", "gpsimd"))
    kt_v = dma(P, "sync", ktab[:, :], d_ktab, "XC")
    qt_v = dma(P, "sync", qtab[:, :], d_qtab, "XC")

    order = [(g, h) for g in range(NG) for h in range(H)]
    aw_loaded = {}

    def load_aw(oi):
        buf = awb0 if oi % 2 == 0 else awb1
        aw_loaded[oi] = dma(P, "gpsimd", buf[:, :], d_aw[order[oi][1]], f"A{oi % 2}")

    load_aw(0)
    load_aw(1)

    tri_v = None
    pending_tail = None
    dve_evac = {"v": 0}
    wo_res_final = {}

    def emit_tail(pt):
        th, trec = pt
        P.wait("tensor", "DVE", trec)
        mm(P, ps[7][:, :], ones_f[0:1, 0:128], irow[:, :], start=True, stop=True,
           inc=("PE", 1))
        P.wait("vector", "PE", P.n["PE"])
        vop(P, "tensor_copy", invlb[:, :], ps[7][:, :], inc=("DVE", 1))
        P.wait("vector", "DVE", P.n["DVE"])
        r0 = th * 192
        t0, o0 = divmod(r0, 128)
        if o0 == 0:
            vop(P, "tensor_mul", aog[:, t0 * 512:(t0 + 1) * 512], ps[2][:, :],
                invlb[:, :])
            vop(P, "tensor_mul", aog[0:64, (t0 + 1) * 512:(t0 + 2) * 512],
                ps[3][0:64, :], invlb[0:64, :], inc=("DVE", 1))
        else:
            vop(P, "tensor_mul", aog[64:128, t0 * 512:(t0 + 1) * 512],
                ps[2][0:64, :], invlb[0:64, :])
            vop(P, "tensor_mul", aog[0:64, (t0 + 1) * 512:(t0 + 2) * 512],
                ps[2][64:128, :], invlb[64:128, :])
            vop(P, "tensor_mul", aog[64:128, (t0 + 1) * 512:(t0 + 2) * 512],
                ps[3][0:64, :], invlb[0:64, :], inc=("DVE", 1))
        dve_evac["v"] = P.n["DVE"]


    bank45_guard = [("ACT", 0), ("ACT", 0)]
    vproj_guard = {}
    pe_attn_hist = {-1: 0, -2: 0}
    dve_q_hist = {-1: 0}
    act_q_hist = {-1: 0}

    def vproj_chunks(oi2):
        g2, h2 = order[oi2]
        awb2 = awb0 if oi2 % 2 == 0 else awb1
        vbuf = vh if oi2 % 2 == 0 else vh2
        NTBP2 = EXT[g2] // 2
        guards = []
        AWV2 = 3072
        for tbp in range(NTBP2):
            bank = 4 + (tbp % 2)
            if tbp == 0:
                P.wait("tensor", f"A{oi2 % 2}", aw_loaded[oi2])
            if tbp < 2:
                P.wait("tensor", *bank45_guard[tbp % 2])
                if bank == 5:
                    P.wait("tensor", "ACT", act_q_hist[oi2 - 1])
            else:
                P.wait("tensor", *guards[tbp - 2])
            for sub in range(2):
                tb = tbp * 2 + sub
                for kc in range(KC):
                    mm(P, ps[bank][:, sub * 192:sub * 192 + 192],
                       kvlat[:, kc * 2048 + tb * 128: kc * 2048 + (tb + 1) * 128],
                       awb2[:, AWV2 + kc * 192: AWV2 + (kc + 1) * 192],
                       start=(kc == 0), stop=(kc == KC - 1),
                       inc=("PE", 1) if (kc == KC - 1 and sub == 1) else None)
            pvv = P.n["PE"]
            P.wait("scalar", "PE", pvv)
            P.wait("scalar", "PE", pe_attn_hist[oi2 - 2])
            acopy(P, vbuf[:, (tbp * 2) * VST:(tbp * 2) * VST + 192],
                  ps[bank][:, 0:192])
            acopy(P, vbuf[:, (tbp * 2 + 1) * VST:(tbp * 2 + 1) * VST + 192],
                  ps[bank][:, 192:384], inc=("ACT", 1))
            guards.append(("ACT", P.n["ACT"]))
            bank45_guard[tbp % 2] = guards[-1]
            vproj_guard[oi2] = list(guards)
            yield

    woven = {}

    for oi, (g, h) in enumerate(order):
        awb = awb0 if oi % 2 == 0 else awb1
        AWQ, AWK, AWV = 0, 2304, 3072
        if h == 0:
            P.wait("sync", "DVE", P.n["DVE"])
            tri_v = dma(P, "sync", tri_g[:, :], d_tri[g], "XC")
            if oi > 0:
                P.wait("gpsimd", "PE", P.n["PE"])
                load_aw(oi)
                load_aw(oi + 1)
            P.wait("vector", "PE", P.n["PE"])
            for tb in range(NKB):
                vop(P, "memset", vh[:, tb * VST + 192: tb * VST + 193], 1.0)
                vop(P, "memset", vh2[:, tb * VST + 192: tb * VST + 193], 1.0,
                    inc=("DVE", 1) if tb == NKB - 1 else None)
            P.wait("tensor", "DVE", P.n["DVE"])
            P.wait("vector", "XC", tri_v)
            P.wait("tensor", "ACT", max(actkvO, actkvE, actql))

        NGRP = EXT[g] // 4
        vbuf_self = vh if oi % 2 == 0 else vh2

        if not woven.get(oi):
            for _ in vproj_chunks(oi):
                pass

        act_head0 = P.n["ACT"]
        dve_head0 = P.n["DVE"]
        act_k_ev = {}
        dve_k = {}
        for ng in range(NGRP):
            bn = 5 if ng % 2 == 0 else 0
            br = 6 if ng % 2 == 0 else 1
            if ng % 2 == 0:
                if ng == 0:
                    for g_ in vproj_guard[oi]:
                        P.wait("tensor", *g_)
                    P.wait("tensor", *bank45_guard[1])
                    P.wait("tensor", "DVE", dve_q_hist[oi - 1])
                else:
                    P.wait("tensor", "ACT", act_k_ev[ng - 2])
                    P.wait("tensor", "DVE", dve_k[ng - 2])
            else:
                if ng == 1:
                    P.wait("tensor", "ACT", act_head0)
                    P.wait("tensor", "DVE", dve_head0)
                else:
                    P.wait("tensor", "ACT", act_k_ev[ng - 2])
                    P.wait("tensor", "DVE", dve_k[ng - 2])
            for kc in range(KC):
                mm(P, ps[bn][:, :],
                   awb[:, AWK + kc * 192: AWK + kc * 192 + 128],
                   kvlat[:, kc * 2048 + ng * 512: kc * 2048 + (ng + 1) * 512],
                   start=(kc == 0), stop=(kc == KC - 1))
                mm(P, ps[br][0:64, :],
                   awb[:, AWK + kc * 192 + 128: AWK + (kc + 1) * 192],
                   kvlat[:, kc * 2048 + ng * 512: kc * 2048 + (ng + 1) * 512],
                   start=(kc == 0), stop=(kc == KC - 1),
                   inc=("PE", 1) if kc == KC - 1 else None)
            pk = P.n["PE"]
            P.wait("scalar", "PE", pk)
            acopy(P, kh_a[:, ng * 512:(ng + 1) * 512], ps[bn][:, :], inc=("ACT", 1))
            act_k_ev[ng] = P.n["ACT"]
            P.wait("vector", "PE", pk)
            if ng >= 1:
                P.wait("vector", "DVE", dve_k[ng - 1])
            vop(P, "tensor_mul", ropeA[:, :], ps[br][0:64, :],
                ktab[0:64, ng * 512:(ng + 1) * 512])
            vop(P, "tensor_mul", ropeB[0:32, :], ps[br][32:64, :],
                ktab[64:96, ng * 512:(ng + 1) * 512])
            vop(P, "tensor_mul", ropeB[32:64, :], ps[br][0:32, :],
                ktab[64:96, ng * 512:(ng + 1) * 512], inc=("DVE", 1))
            P.wait("vector", "DVE", P.n["DVE"])
            vop(P, "tensor_sub", kh_b[0:32, ng * 512:(ng + 1) * 512],
                ropeA[0:32, :], ropeB[0:32, :])
            vop(P, "tensor_add", kh_b[32:64, ng * 512:(ng + 1) * 512],
                ropeA[32:64, :], ropeB[32:64, :], inc=("DVE", 1))
            dve_k[ng] = P.n["DVE"]

        P.wait("tensor", "ACT", act_k_ev[max(0, NGRP - 2)])
        P.wait("tensor", "DVE", dve_k[max(0, NGRP - 2)])
        for kc in range(QC):
            mm(P, ps[5][:, :], awb[:, AWQ + kc * 192: AWQ + kc * 192 + 128],
               qlat[:, kc * TOK + g * 512: kc * TOK + (g + 1) * 512],
               start=(kc == 0), stop=(kc == QC - 1))
            mm(P, ps[6][0:64, :], awb[:, AWQ + kc * 192 + 128: AWQ + (kc + 1) * 192],
               qlat[:, kc * TOK + g * 512: kc * TOK + (g + 1) * 512],
               start=(kc == 0), stop=(kc == QC - 1),
               inc=("PE", 1) if kc == QC - 1 else None)
        pq = P.n["PE"]
        P.wait("scalar", "PE", pq)
        acopy(P, qh_a[:, :], ps[5][:, :], inc=("ACT", 1))
        act_q_ev = P.n["ACT"]
        act_q_hist[oi] = act_q_ev
        P.wait("vector", "PE", pq)
        P.wait("vector", "DVE", dve_k[NGRP - 1])
        vop(P, "tensor_mul", ropeA[:, :], ps[6][0:64, :], qtab[0:64, g * 512:(g + 1) * 512])
        vop(P, "tensor_mul", ropeB[0:32, :], ps[6][32:64, :], qtab[64:96, g * 512:(g + 1) * 512])
        vop(P, "tensor_mul", ropeB[32:64, :], ps[6][0:32, :], qtab[64:96, g * 512:(g + 1) * 512],
            inc=("DVE", 1))
        P.wait("vector", "DVE", P.n["DVE"])
        vop(P, "tensor_sub", qh_b[0:32, :], ropeA[0:32, :], ropeB[0:32, :])
        vop(P, "tensor_add", qh_b[32:64, :], ropeA[32:64, :], ropeB[32:64, :],
            inc=("DVE", 1))
        dve_q = P.n["DVE"]
        dve_q_hist[oi] = dve_q

        if pending_tail is not None:
            emit_tail(pending_tail)
            pending_tail = None

        exp_done = {}
        pdone = {}

        def emit_scores(kb):
            sb = kb % 2
            if kb == 0:
                P.wait("tensor", "ACT", act_q_ev)
            if kb >= 2:
                P.wait("tensor", "ACT", exp_done[kb - 2])
            mm(P, ps[sb][:, :], kh_a[:, kb * 128:(kb + 1) * 128], qh_a[:, :],
               start=True, stop=False)
            if kb == 0:
                P.wait("tensor", "DVE", dve_q)
            mm(P, ps[sb][:, :], kh_b[:, kb * 128:(kb + 1) * 128], qh_b[:, :],
               start=False, stop=True, inc=("PE", 1))
            psc = P.n["PE"]
            P.wait("scalar", "PE", psc)
            pt = ptile[:, (kb % 3) * 512:(kb % 3) * 512 + 512]
            act(P, pt, ps[sb][:, :], AF.Exp,
                bias=t_badd[:, g * 16 + kb: g * 16 + kb + 1], scale=SCALE,
                inc=("ACT", 1))
            exp_done[kb] = P.n["ACT"]
            if DIAG[g] <= kb < DIAG[g] + 4:
                jj = kb - DIAG[g]
                P.wait("vector", "ACT", exp_done[kb])
                vop(P, "tensor_mul", pt, pt, tri_g[:, jj * 512:(jj + 1) * 512],
                    inc=("DVE", 1))
                pdone[kb] = ("DVE", P.n["DVE"])
            else:
                pdone[kb] = ("ACT", exp_done[kb])

        def emit_pv(kb):
            st, v = pdone[kb]
            P.wait("tensor", st, v)
            if kb == 0:
                P.wait("tensor", "DVE", dve_evac["v"])
                for g_ in vproj_guard[oi]:
                    P.wait("tensor", *g_)
            pt = ptile[:, (kb % 3) * 512:(kb % 3) * 512 + 512]
            mm(P, ps[2][:, :], vbuf_self[:, kb * VST: kb * VST + 128], pt,
               start=(kb == 0), stop=(kb == EXT[g] - 1))
            mm(P, ps[3][0:65, :], vbuf_self[:, kb * VST + 128: kb * VST + 193], pt,
               start=(kb == 0), stop=(kb == EXT[g] - 1),
               inc=("PE", 1) if kb == EXT[g] - 1 else None)

        weave = None
        if oi + 1 < len(order) and (oi + 1) % H != 0:
            weave = vproj_chunks(oi + 1)
            woven[oi + 1] = True

        def step_weave():
            if weave is not None:
                next(weave, None)

        for _ in range(4):
            step_weave()

        emit_scores(0)
        emit_scores(1)
        for kb in range(2, EXT[g]):
            emit_scores(kb)
            step_weave()
            emit_pv(kb - 2)
        step_weave()
        emit_pv(EXT[g] - 2)
        step_weave()
        emit_pv(EXT[g] - 1)
        if weave is not None:
            for _ in weave:
                pass
        pe_attn = P.n["PE"]
        pe_attn_hist[oi] = pe_attn

        if oi + 2 < len(order) and (oi + 2) % H not in (0, 1):
            P.wait("gpsimd", "PE", pe_attn)
            load_aw(oi + 2)

        P.wait("vector", "PE", pe_attn)
        vop(P, "tensor_copy", lrow[:, :], ps[3][64:65, :], inc=("DVE", 1))
        P.wait("vector", "DVE", P.n["DVE"])
        vop(P, "reciprocal", irow[:, :], lrow[:, :], inc=("DVE", 1))
        pending_tail = (h, P.n["DVE"])
        if h == H - 1:
            emit_tail(pending_tail)
            pending_tail = None

        if h == H - 1:
            dve_aog = P.n["DVE"]
            wo_res = {}
            P.wait("gpsimd", "PE", P.n["PE"])
            P.wait("gpsimd", "DVE", P.n["DVE"])
            wo_v = {}
            wo_pe = {}

            wo_bufs = (wob, wob1, wob2)
            wo_sems = ("WO", "W1", "W2")

            def load_wo(mc):
                wo_v[mc] = dma(P, "gpsimd", wo_bufs[mc % 3][:, :], d_wo[mc],
                               wo_sems[mc % 3])

            load_wo(0)
            load_wo(1)
            load_wo(2)
            for mc in range(FC):
                wbuf = wo_bufs[mc % 3]
                P.wait("tensor", wo_sems[mc % 3], wo_v[mc])
                if mc == 0:
                    P.wait("tensor", "DVE", dve_aog)
                bank = mc % 4
                if mc >= 4:
                    P.wait("tensor", "DVE", wo_res[mc - 4])
                for kc in range(24):
                    mm(P, ps[bank][:, :], wbuf[:, kc * 128:(kc + 1) * 128],
                       aog[:, kc * 512:(kc + 1) * 512],
                       start=(kc == 0), stop=(kc == 23),
                       inc=("PE", 1) if kc == 23 else None)
                pwo = P.n["PE"]
                wo_pe[mc] = pwo
                if mc + 3 < FC:
                    P.wait("gpsimd", "PE", pwo)
                    load_wo(mc + 3)
                P.wait("vector", "PE", pwo)
                dst = resid[:, mc * TOK + g * 512: mc * TOK + g * 512 + 512]
                vop(P, "tensor_add", dst, ps[bank][:, :], dst, inc=("DVE", 1))
                wo_res[mc] = P.n["DVE"]
                if g == NG - 1:
                    wo_res_final[mc] = P.n["DVE"]

    if upto == "attn":
        finish(resid)
        return _finalize(nc, P, root)

    preF = emit_ffn_preload(gbuf2, ubuf2, "read")
    b5 = emit_ln(None, resid, xn3, abc2, bbc2, "sbuf",
                 ready={fc: ("DVE", wo_res_final[fc]) for fc in range(FC)})
    emit_ffn(xn3, hdn2, gbuf2, ubuf2, dbuf2, [0], None, "inplace", b5, wc="read",
             pre=preF)
    emit_ffn(xn3, hdn2, gbuf2, ubuf2, dbuf2, [1], None, "inplace", b5, wc="read")
    del preF

    finish(resid)
    return _finalize(nc, P, root)


def _finalize(nc, P, root):
    with nc.Block() as block:
        @block.sync
        def _(e):
            for f in P.ops["sync"]:
                f(e)

        @block.gpsimd
        def _(e):
            for f in P.ops["gpsimd"]:
                f(e)

        @block.tensor
        def _(e):
            for f in P.ops["tensor"]:
                f(e)

        @block.vector
        def _(e):
            for f in P.ops["vector"]:
                f(e)

        @block.scalar
        def _(e):
            for f in P.ops["scalar"]:
                f(e)

    root.close()
    return nc



def _prep_shared(inp):
    f32 = lambda a: np.ascontiguousarray(np.asarray(a, dtype=np.float32))
    gate, up, down = f32(inp["L0_ffn_gate_w"]), f32(inp["L0_ffn_up_w"]), f32(inp["L0_ffn_down_w"])
    gw = gate.T.reshape(FC, 128, IC, 128).transpose(2, 1, 0, 3).reshape(IC, 128, FC * 128)
    uw = up.T.reshape(FC, 128, IC, 128).transpose(2, 1, 0, 3).reshape(IC, 128, FC * 128)
    dw = down.T.reshape(IC, 128, FC, 128).transpose(2, 1, 0, 3).reshape(FC, 128, IC * 128)
    gsc = f32(inp["L0_ffn_gate_s"]).reshape(IC, 128).T
    usc = f32(inp["L0_ffn_up_s"]).reshape(IC, 128).T
    dsc = f32(inp["L0_ffn_down_s"]).reshape(FC, 128).T
    wqd = f32(inp["L0_attn_wq_down"]).reshape(FC, 128, QC, 128).transpose(2, 1, 0, 3).reshape(QC, 128, FC * 128)
    wkvd = f32(inp["L0_attn_wkv_down"]).reshape(FC, 128, KC, 128).transpose(2, 1, 0, 3).reshape(KC, 128, FC * 128)
    wo = f32(inp["L0_attn_wo"]).reshape(24, 128, FC, 128).transpose(2, 1, 0, 3).reshape(FC, 128, 24 * 128)
    wqn, wqr = f32(inp["L0_attn_wq_nope"]), f32(inp["L0_attn_wq_rope"])
    wkn, wkr = f32(inp["L0_attn_wk_nope"]), f32(inp["L0_attn_wk_rope"])
    wvu = f32(inp["L0_attn_wv_up"])
    perm = np.concatenate([np.arange(0, 64, 2), np.arange(1, 64, 2)])
    aw = np.empty((H, 128, 3840), dtype=np.float32)
    for h in range(H):
        q = np.concatenate([wqn[:, h * 128:(h + 1) * 128],
                            wqr[:, h * 64:(h + 1) * 64][:, perm]], axis=1)
        k = np.concatenate([wkn[:, h * 128:(h + 1) * 128],
                            wkr[:, h * 64:(h + 1) * 64][:, perm]], axis=1)
        v = wvu[:, h * 192:(h + 1) * 192]
        aw[h, :, 0:2304] = q.reshape(QC, 128, 192).transpose(1, 0, 2).reshape(128, 2304)
        aw[h, :, 2304:3072] = k.reshape(KC, 128, 192).transpose(1, 0, 2).reshape(128, 768)
        aw[h, :, 3072:3840] = v.reshape(KC, 128, 192).transpose(1, 0, 2).reshape(128, 768)
    return dict(gw=gw, uw=uw, dw=dw, gsc=np.ascontiguousarray(gsc),
                usc=np.ascontiguousarray(usc), dsc=np.ascontiguousarray(dsc),
                wqd=wqd, wkvd=wkvd, wo=wo, aw=aw)


def _rope_tab(pos, width):
    inv = 1.0 / (10000.0 ** (np.arange(0, DR, 2, dtype=np.float64) / DR))
    ang = np.outer(pos.astype(np.float64), inv)
    cos = np.cos(ang).T.astype(np.float32)
    sin = np.sin(ang).T.astype(np.float32)
    tab = np.zeros((128, width), dtype=ml_dtypes.bfloat16)
    n = pos.shape[0]
    tab[0:32, :n] = cos
    tab[32:64, :n] = cos
    tab[64:96, :n] = sin
    return tab


def _core_layout(s):
    q = [np.arange(i * 512, (i + 1) * 512) for i in range(4)]
    if s == 0:
        pos_q = np.concatenate([q[0], q[3]])
        pos_e = np.concatenate([q[1], q[2]])
    else:
        pos_q = np.concatenate([q[1], q[2]])
        pos_e = np.concatenate([q[0], q[3]])
    pos_k = np.concatenate([pos_q[0:512], pos_e, pos_q[512:1024]])
    return pos_q, pos_e, pos_k


def _prep_core(inp, core):
    b, s = core // 2, core % 2
    x = np.asarray(inp["L0_attn_x"], dtype=np.float32)
    pos_q, pos_e, pos_k = _core_layout(s)
    own = np.ascontiguousarray(x[b][pos_q])
    extra = np.ascontiguousarray(x[b][pos_e])
    xo = np.ascontiguousarray(own.T.reshape(FC, 128, TOK))
    xe = np.ascontiguousarray(extra.T.reshape(FC, 128, TOK))
    ktab = _rope_tab(pos_k, 2048)
    qtab = _rope_tab(pos_q, TOK)
    tri = np.zeros((NG, 128, 4 * TGS), dtype=ml_dtypes.bfloat16)
    badd = np.zeros((128, 32), dtype=np.float32)
    for g in range(NG):
        qp = pos_q[g * TGS:(g + 1) * TGS]
        for kb in range(EXT[g]):
            kp = pos_k[kb * 128:(kb + 1) * 128]
            vis = kp[:, None] <= qp[None, :]
            if DIAG[g] <= kb < DIAG[g] + 4:
                jj = kb - DIAG[g]
                tri[g, :, jj * TGS:(jj + 1) * TGS] = vis.astype(ml_dtypes.bfloat16)
                badd[:, g * 16 + kb] = -MSHIFT
            elif not vis.any():
                badd[:, g * 16 + kb] = NEGBIG
            else:
                assert vis.all(), f"core{core} g{g} kb{kb}: partial mask outside diag"
                badd[:, g * 16 + kb] = -MSHIFT
    return dict(xo=xo, xe=xe, ktab=ktab, qtab=qtab, tri=tri, badd=badd)


def _assemble(results):
    out = np.empty((B, T, C), dtype=np.float32)
    for core in range(8):
        b, s = core // 2, core % 2
        pos_q, _, _ = _core_layout(s)
        y = np.asarray(results[core]["y"], dtype=np.float32)
        out[b, pos_q] = y.reshape(C, TOK).T
    return out


_NC_CACHE = {}


def kernel(**inputs):
    shared = _prep_shared(inputs)
    in_maps = []
    for core in range(8):
        m = dict(shared)
        m.update(_prep_core(inputs, core))
        in_maps.append(m)
    if "prog" not in _NC_CACHE:
        _NC_CACHE["prog"] = build_program()
    res = run_bass_kernel_spmd(_NC_CACHE["prog"], in_maps, core_ids=list(range(8)))
    return _assemble(res.results)
